# version 31
# baseline (speedup 1.0000x reference)
import contextlib
import numpy as np
import ml_dtypes
import concourse.bass as bass
import concourse.mybir as mybir
from concourse.bass_utils import run_bass_kernel_spmd

F32 = mybir.dt.float32
BF16 = mybir.dt.bfloat16
AF = mybir.ActivationFunctionType
ALU = mybir.AluOpType

T = 4096
C = 1024
NMACRO = 8
NPAIR = 8
NEGC = -30.0
SQK = 1.0 / 8192.0
SPV = 1.0 / 1024.0

ENGS = ("sync", "tensor", "vector", "scalar", "gpsimd")

DMA_SEMS = (["mA", "wL", "lre", "rls", "tp", "os2"]
            + [f"{nm}{i}" for nm in ("xL", "pr", "bc", "ls", "os", "of", "xg", "ys", "xs", "xf", "rb")
               for i in range(2)]
            + [f"sg{i}" for i in range(4)] + ["ls2", "xs2", "xs3"])
COMPUTE_SEMS = ["sP", "sV", "sC", "sG"]


class Prog:

    def __init__(self):
        self.ops = {e: [] for e in ENGS}
        self.cnt = {}
        self.guar = {e: {} for e in ENGS}

    def add_sem(self, name):
        self.cnt[name] = 0

    def sem_names(self):
        return list(self.cnt.keys())

    def emit(self, eng, fn, waits=(), inc=None):
        w = []
        for sem, val in waits:
            if val <= 0:
                continue
            assert val <= self.cnt[sem], f"wait on future token {sem}={val}>{self.cnt[sem]}"
            if self.guar[eng].get(sem, 0) >= val:
                continue
            self.guar[eng][sem] = val
            w.append((sem, val))
        if inc is not None:
            sem, n = inc
            self.cnt[sem] += n
            if n < 16 and eng != "sync":
                self.guar[eng][sem] = max(self.guar[eng].get(sem, 0), self.cnt[sem])
        self.ops[eng].append((fn, w, inc))

    def tok(self, sem):
        return (sem, self.cnt[sem])

    def run(self, eng_name, eng, sems):
        for fn, waits, inc in self.ops[eng_name]:
            for sem, val in waits:
                eng.wait_ge(sems[sem], val)
            ins = fn(eng, sems)
            if inc is not None:
                assert ins is not None, f"inc {inc} on op returning None ({eng_name})"
                ins.then_inc(sems[inc[0]], inc[1])


def build_program(debug_taps=False):
    nc = bass.Bass("TRN2", target_bir_lowering=False, debug=False)

    x_d = nc.dram_tensor("x", [T, C], F32, kind="ExternalInput").ap()
    w_d = {}
    for nm in ["wq", "wk", "wv", "wp", "wr", "wk2", "wv2"]:
        w_d[nm] = nc.dram_tensor(nm, [C, C], F32, kind="ExternalInput").ap()
    mur_d = nc.dram_tensor("mu_r", [C], F32, kind="ExternalInput").ap()
    muk_d = nc.dram_tensor("mu_k", [C], F32, kind="ExternalInput").ap()
    identb_d = nc.dram_tensor("identb", [128, 128], BF16, kind="ExternalInput").ap()
    identf_d = nc.dram_tensor("identf", [128, 128], F32, kind="ExternalInput").ap()
    tri_d = nc.dram_tensor("tri", [128, 128], BF16, kind="ExternalInput").ap()
    y_d = nc.dram_tensor("y", [T, C], F32, kind="ExternalOutput").ap()
    v_scr = nc.dram_tensor("v_scr", [NPAIR, 2, 32, 128, 64], BF16, kind="Internal").ap()
    q_scr = nc.dram_tensor("q_scr", [NPAIR, 128, 4096], BF16, kind="Internal").ap()
    k_scr = nc.dram_tensor("k_scr", [NPAIR, 128, 4096], BF16, kind="Internal").ap()
    o_scr = nc.dram_tensor("o_scr", [NPAIR, 128, T], BF16, kind="Internal").ap()
    rl_scr = nc.dram_tensor("rl_scr", [NPAIR, 8192], F32, kind="Internal").ap()
    taps = {}
    if debug_taps:
        taps["t_v"] = nc.dram_tensor("t_v", [NPAIR, 2, 32, 128, 64], BF16, kind="ExternalOutput").ap()
        taps["t_o"] = nc.dram_tensor("t_o", [NPAIR, 128, T], BF16, kind="ExternalOutput").ap()
        taps["t_l"] = nc.dram_tensor("t_l", [NPAIR, 8192], F32, kind="ExternalOutput").ap()
        taps["t_z"] = nc.dram_tensor("t_z", [128, 1024], F32, kind="ExternalOutput").ap()
        taps["t_att"] = nc.dram_tensor("t_att", [128, 1024], F32, kind="ExternalOutput").ap()
        taps["t_xre"] = nc.dram_tensor("t_xre", [128, 1024], F32, kind="ExternalOutput").ap()
        taps["t_stats"] = nc.dram_tensor("t_stats", [128, 28], F32, kind="ExternalOutput").ap()
        taps["t_y1"] = nc.dram_tensor("t_y1", [128, 4096], F32, kind="ExternalOutput").ap()
        taps["t_xr"] = nc.dram_tensor("t_xr", [128, 4096], BF16, kind="ExternalOutput").ap()
        taps["t_kact"] = nc.dram_tensor("t_kact", [128, 4096], BF16, kind="ExternalOutput").ap()
        taps["t_r"] = nc.dram_tensor("t_r", [128, 1024], F32, kind="ExternalOutput").ap()
        taps["t_z2"] = nc.dram_tensor("t_z2", [128, 1024], F32, kind="ExternalOutput").ap()

    p = Prog()
    for s in DMA_SEMS + COMPUTE_SEMS:
        p.add_sem(s)
    NOT = ("sP", 0)

    def tap_barrier():
        tokb = p.tok("tp")
        for en in ENGS:
            p.emit(en, lambda e, s, tokb=tokb: e.wait_ge(s[tokb[0]], tokb[1]))

    with contextlib.ExitStack() as st0:
        E0 = st0.enter_context
        identb = E0(nc.sbuf_tensor("s_identb", [128, 128], BF16))
        identf = E0(nc.sbuf_tensor("s_identf", [128, 128], F32))
        tri = E0(nc.sbuf_tensor("s_tri", [128, 128], BF16))
        mur_fm = E0(nc.sbuf_tensor("mur_fm", [128, 8], F32))
        muk_fm = E0(nc.sbuf_tensor("muk_fm", [128, 8], F32))
        consts = E0(nc.sbuf_tensor("consts", [128, 4], F32))
        wT = {}
        for nm in ["wp", "wr", "wk2", "wv2"]:
            wT[nm] = E0(nc.sbuf_tensor("T_" + nm, [128, 8192], BF16))

        def c_negC(): return consts[:, 0:1]
        def c_sqk(): return consts[:, 1:2]
        def c_eps(): return consts[:, 2:3]
        def c_nh(): return consts[:, 3:4]

        p.emit("sync", lambda e, s: e.dma_start(out=identb[:], in_=identb_d), inc=("mA", 16))
        p.emit("sync", lambda e, s: e.dma_start(out=identf[:], in_=identf_d), inc=("mA", 16))
        p.emit("sync", lambda e, s: e.dma_start(out=tri[:], in_=tri_d), inc=("mA", 16))
        tok_consts = p.tok("mA")
        p.emit("vector", lambda e, s: e.memset(consts[:, 0:1], NEGC))
        p.emit("vector", lambda e, s: e.memset(consts[:, 1:2], SQK))
        p.emit("vector", lambda e, s: e.memset(consts[:, 2:3], 1e-5))
        p.emit("vector", lambda e, s: e.memset(consts[:, 3:4], -0.5), inc=("sV", 1))
        tok_cmemset = p.tok("sV")

        with contextlib.ExitStack() as stAB:
            EAB = stAB.enter_context
            wTq = EAB(nc.sbuf_tensor("T_wq", [128, 8192], BF16))
            wTk = EAB(nc.sbuf_tensor("T_wk", [128, 8192], BF16))
            o_stage = [EAB(nc.sbuf_tensor(f"o_stage{i}", [128, 4096], BF16)) for i in range(3)]
            l_stage = [EAB(nc.sbuf_tensor(f"l_stage{i}", [1, 512], F32)) for i in range(3)]

            with contextlib.ExitStack() as st1:
                E1 = st1.enter_context
                wnat = E1(nc.sbuf_tensor("wnat", [128, 8192], BF16))
                wTv = E1(nc.sbuf_tensor("T_wv", [128, 8192], BF16))
                x_bf = [E1(nc.sbuf_tensor("x_bf0", [128, 4096], BF16))] * 2
                x_fm = [E1(nc.sbuf_tensor(f"x_fm{i}", [128, 4096], BF16)) for i in range(2)]
                mu8 = E1(nc.sbuf_tensor("mu8", [8, 128], F32))
                stg = [E1(nc.sbuf_tensor(f"stg{i}", [128, 512], BF16)) for i in range(4)]
                qks = [E1(nc.sbuf_tensor(f"qks{i}", [128, 512], BF16)) for i in range(4)]
                pV = [E1(nc.psum_tensor(f"pV{i}", [128, 512], F32)) for i in range(3)]
                pT = [E1(nc.psum_tensor(f"pT{i}", [128, 1024], BF16)) for i in range(2)]
                pQK = [E1(nc.psum_tensor(f"pQK{i}", [128, 512], F32)) for i in range(3)]
                pF = pT[0][:, 0:16].bitcast(F32)

                p.emit("sync", lambda e, s: e.dma_start(
                    out=mu8[:], in_=mur_d.rearrange("(c p) -> c p", p=128)), inc=("mA", 16))
                tok = p.tok("mA")
                p.emit("tensor", lambda e, s: e.transpose(out=pF, in_=mu8[:], identity=identf[0:8, 0:8]),
                       waits=[tok, tok_consts], inc=("sP", 1))
                tokp = p.tok("sP")
                p.emit("vector", lambda e, s: e.tensor_copy(out=mur_fm[:], in_=pF),
                       waits=[tokp], inc=("sV", 1))
                tokv = p.tok("sV")
                p.emit("sync", lambda e, s: e.dma_start(
                    out=mu8[:], in_=muk_d.rearrange("(c p) -> c p", p=128)),
                    waits=[tokp], inc=("mA", 16))
                tok2 = p.tok("mA")
                p.emit("tensor", lambda e, s: e.transpose(out=pF, in_=mu8[:], identity=identf[0:8, 0:8]),
                       waits=[tok2, tokv], inc=("sP", 1))
                tokp2 = p.tok("sP")
                p.emit("vector", lambda e, s: e.tensor_copy(out=muk_fm[:], in_=pF),
                       waits=[tokp2], inc=("sV", 1))
                tok_mu_evac = p.tok("sV")

                evac_tok = [tok_mu_evac, NOT]
                wnat_free = NOT
                wload_tok = NOT
                wready = {}
                wdsts = {"wq": wTq, "wk": wTk, "wv": wTv, "wp": wT["wp"], "wr": wT["wr"],
                         "wk2": wT["wk2"], "wv2": wT["wv2"]}

                def wprep_load(nm):
                    nonlocal wload_tok
                    wdram = w_d[nm]
                    for hf in range(2):
                        def wload(e, s, wdram=wdram, hf=hf):
                            return e.dma_start(
                                out=wnat[:, hf * 4096:(hf + 1) * 4096].rearrange(
                                    "p (oc c) -> p oc c", c=1024),
                                in_=wdram[hf * 512:(hf + 1) * 512, :].rearrange(
                                    "(oc p) c -> oc p c", p=128).transpose([1, 0, 2]))
                        p.emit("gpsimd", wload, waits=[wnat_free], inc=("wL", 16))
                    wload_tok = p.tok("wL")

                def wprep_tr(nm):
                    nonlocal wnat_free
                    dst = wdsts[nm]
                    for oc in range(8):
                        bank = pT[oc % 2]
                        for cc in range(8):
                            def mmf(e, s, oc=oc, cc=cc, bank=bank):
                                return e.transpose(out=bank[:, cc * 128:(cc + 1) * 128],
                                                   in_=wnat[:, oc * 1024 + cc * 128: oc * 1024 + (cc + 1) * 128],
                                                   identity=identb[:])
                            p.emit("tensor", mmf,
                                   waits=[tok_consts, wload_tok, evac_tok[oc % 2]],
                                   inc=("sP", 1) if cc == 7 else None)
                        tok_tr = p.tok("sP")
                        def evacf(e, s, dst=dst, oc=oc, bank=bank):
                            dv = dst[:, :].rearrange("p (cc x) -> p cc x", x=1024)[:, :, oc * 128:(oc + 1) * 128]
                            return e.tensor_copy(out=dv, in_=bank[:].rearrange("p (cc x) -> p cc x", x=128))
                        p.emit("vector", evacf, waits=[tok_tr], inc=("sV", 1))
                        evac_tok[oc % 2] = p.tok("sV")
                    wnat_free = p.tok("sP")
                    wready[nm] = p.tok("sV")

                WORDER = ["wv", "wq", "wk", "wp", "wr", "wk2", "wv2"]
                wprep_load("wv")
                wprep_tr("wv")

                xcast_tok = [[NOT, NOT], [NOT, NOT]]
                xfm_vdone = [NOT, NOT]
                xbf_trdone = [NOT, NOT]
                stg_tok = [NOT] * 4
                pv_tok = [NOT] * 3
                xfm_ready_arr = [NOT, NOT]
                qkrot = [0]
                qk0_toks = {}
                toks_V_stores = []
                pqk_free = [NOT] * 3
                qks_free = [NOT] * 4

                def emit_qk_all(kind, j):
                    wtok = wready["wq" if kind == "q" else "wk"]
                    wTx = wTq if kind == "q" else wTk
                    scr = q_scr if kind == "q" else k_scr
                    xb = j % 2
                    xtok = xfm_ready_arr[xb]
                    for pp in range(NPAIR):
                        bi = qkrot[0] % 3
                        si = qkrot[0] % 4
                        qkrot[0] += 1
                        for cc in range(8):
                            def qmf(e, s, cc=cc, pp=pp, wTx=wTx, xb=xb, bi=bi):
                                return e.matmul(
                                    out=pQK[bi][:],
                                    lhsT=wTx[:, cc * 1024 + pp * 128: cc * 1024 + (pp + 1) * 128],
                                    rhs=x_fm[xb][:, cc * 512:(cc + 1) * 512],
                                    start=(cc == 0), stop=(cc == 7))
                            p.emit("tensor", qmf, waits=[xtok, wtok, pqk_free[bi]],
                                   inc=("sP", 1) if cc == 7 else None)
                        tok_mm = p.tok("sP")
                        def qev(e, s, bi=bi, si=si):
                            return e.copy(out=qks[si][:], in_=pQK[bi][:])
                        p.emit("scalar", qev, waits=[tok_mm, qks_free[si]], inc=("sC", 1))
                        tok_ev = p.tok("sC")
                        pqk_free[bi] = tok_ev
                        def qst(e, s, scr=scr, pp=pp, j=j, si=si):
                            return e.dma_start(out=scr[pp][:, j * 512:(j + 1) * 512],
                                               in_=qks[si][:])
                        p.emit("scalar", qst, waits=[tok_ev], inc=(f"xs{si}", 16))
                        qks_free[si] = p.tok(f"xs{si}")
                        if pp == 0 and j == NMACRO - 1:
                            qk0_toks[kind] = [p.tok(f"xs{i}") for i in range(4)]
                    xfm_vdone[xb] = p.tok("sP")

                for m in range(NMACRO):
                    buf = m % 2
                    if m == 2:
                        emit_qk_all("k", 0)
                    elif m == 3:
                        emit_qk_all("k", 1)
                    xl_sems = (f"xL{buf}", f"bc{buf}")
                    cast_tok_h = [None, None]
                    for hf in range(2):
                        def xload(e, s, m=m, buf=buf, hf=hf):
                            return e.dma_start(
                                out=x_bf[buf][:, hf * 2048:(hf + 1) * 2048].rearrange(
                                    "p (ts c) -> p ts c", c=1024),
                                in_=x_d[m * 512 + hf * 256: m * 512 + (hf + 1) * 256, :].rearrange(
                                    "(ts p) c -> ts p c", p=128).transpose([1, 0, 2]))
                        p.emit("gpsimd", xload, waits=[xbf_trdone[1 - buf]],
                               inc=(xl_sems[hf], 16))
                        cast_tok_h[hf] = p.tok(xl_sems[hf])
                    xcast_tok[buf] = list(cast_tok_h)
                    if m + 1 < len(WORDER):
                        wprep_load(WORDER[m + 1])
                    for ts in range(4):
                        bank = pT[ts % 2]
                        for cc in range(8):
                            def trf(e, s, ts=ts, cc=cc, bank=bank, buf=buf):
                                return e.transpose(out=bank[:, cc * 128:(cc + 1) * 128],
                                                   in_=x_bf[buf][:, ts * 1024 + cc * 128: ts * 1024 + (cc + 1) * 128],
                                                   identity=identb[:])
                            p.emit("tensor", trf, waits=[xcast_tok[buf][ts // 2], evac_tok[ts % 2]],
                                   inc=("sP", 1) if cc == 7 else None)
                        tok_tr = p.tok("sP")
                        def evx(e, s, ts=ts, bank=bank, buf=buf):
                            dv = x_fm[buf][:, :].rearrange("p (cc x) -> p cc x", x=512)[:, :, ts * 128:(ts + 1) * 128]
                            return e.tensor_copy(out=dv, in_=bank[:].rearrange("p (cc x) -> p cc x", x=128))
                        p.emit("vector", evx, waits=[tok_tr, xfm_vdone[buf]],
                               inc=("sV", 1))
                        evac_tok[ts % 2] = p.tok("sV")
                    xbf_trdone[buf] = tok_tr
                    xfm_ready = p.tok("sV")
                    xfm_ready_arr[buf] = xfm_ready
                    for ts in range(4):
                        for half in range(2):
                            bi = (ts * 2 + half) % 3
                            sgq = (ts * 2 + half) % 4
                            bank = pV[bi]
                            for cc in range(8):
                                def vmf(e, s, ts=ts, half=half, cc=cc, bank=bank, buf=buf):
                                    return e.matmul(out=bank[:],
                                                    lhsT=x_fm[buf][:, cc * 512 + ts * 128: cc * 512 + (ts + 1) * 128],
                                                    rhs=wTv[:, cc * 1024 + half * 512: cc * 1024 + (half + 1) * 512],
                                                    start=(cc == 0), stop=(cc == 7))
                                p.emit("tensor", vmf, waits=[xfm_ready, pv_tok[bi]],
                                       inc=("sP", 1) if cc == 7 else None)
                            tok_mm = p.tok("sP")
                            sgi = sgq
                            def evv(e, s, bank=bank, sgi=sgi):
                                return e.copy(out=stg[sgi][:], in_=bank[:])
                            p.emit("scalar", evv, waits=[tok_mm, stg_tok[sgi]], inc=("sC", 1))
                            pv_tok[bi] = p.tok("sC")
                            def stv(e, s, ts=ts, half=half, m=m, sgi=sgi):
                                sc = m * 4 + ts
                                dstv = v_scr[half * 4:(half + 1) * 4, :, sc, :, :]
                                dstv = dstv.transpose([2, 0, 1, 3])
                                return e.dma_start(out=dstv, in_=stg[sgi][:].rearrange(
                                    "p (pr h d) -> p pr h d", pr=4, h=2))
                            p.emit("sync", stv, waits=[pv_tok[bi]], inc=(f"sg{sgi}", 16))
                            stg_tok[sgi] = p.tok(f"sg{sgi}")
                    xfm_vdone[buf] = p.tok("sP")
                    if m == NMACRO - 1:
                        toks_V_stores.extend([p.tok(f"sg{i}") for i in range(4)])
                    if m == 1:
                        emit_qk_all("q", 0)
                        emit_qk_all("q", 1)
                    elif m >= 2:
                        emit_qk_all("q", m)
                        emit_qk_all("k", m)
                    if m + 1 < len(WORDER):
                        wprep_tr(WORDER[m + 1])

                toks_A_stores = ([p.tok(f"sg{i}") for i in range(4)]
                                 + [p.tok(f"xs{i}") for i in range(4)])
                tok_A_pe = p.tok("sP")
                tok_A_c = p.tok("sC")

            with contextlib.ExitStack() as st2:
                E2 = st2.enter_context
                Kt = [E2(nc.sbuf_tensor(f"Kt{i}", [128, T], BF16)) for i in range(2)]
                Qt = [E2(nc.sbuf_tensor(f"Qt{i}", [128, 4096], BF16)) for i in range(2)]
                Vt = [E2(nc.sbuf_tensor(f"Vt{i}", [128, 32 * 130], BF16)) for i in range(2)]
                Pbuf = [E2(nc.sbuf_tensor(f"Pbuf{i}", [128, 1536], BF16)) for i in range(8)]
                bc = [E2(nc.sbuf_tensor(f"bc{i}", [128, 512], F32)) for i in range(2)]
                pS = [E2(nc.psum_tensor(f"pS{i}", [128, 1536], F32)) for i in range(2)]
                pPV = [E2(nc.psum_tensor(f"pPV{i}", [128, 512], F32)) for i in range(2)]

                for i in range(2):
                    p.emit("vector", lambda e, s, i=i: e.memset(
                        Vt[i][:, :].rearrange("p (k x) -> p k x", x=65)[:, :, 64:65], 1.0),
                        waits=[tok_A_pe, tok_A_c] + toks_A_stores,
                        inc=("sV", 1))
                tok_ones = p.tok("sV")

                vt_tok = [NOT, NOT]
                pair_free = [tok_A_pe, tok_A_pe]
                qk_done = [[NOT], [NOT]]
                ps_free = [NOT, NOT]
                pbuf_free = [NOT] * 8
                ppv_free = [[NOT], [NOT]]
                ostage_free = [NOT, NOT, NOT]
                lstage_free = [NOT] * 3
                bc_free = [NOT, NOT]
                bc_rot = [0]

                def load_vt(pp):
                    buf = pp % 2
                    w = (toks_V_stores if pp == 0 else toks_A_stores) + [pair_free[buf], tok_ones]
                    for h in range(2):
                        def vload(e, s, pp=pp, h=h, buf=buf):
                            dstv = Vt[buf][:, :].rearrange("p (k x) -> p k x", x=130)[:, :, h * 65: h * 65 + 64]
                            return e.dma_start(out=dstv, in_=v_scr[pp, h].transpose([1, 0, 2]))
                        p.emit("sync", vload, waits=w, inc=(f"pr{buf}", 16))
                    vt_tok[buf] = p.tok(f"pr{buf}")

                def load_qk(pp):
                    buf = pp % 2
                    wq_ = (qk0_toks["q"] if pp == 0 else toks_A_stores) + [pair_free[buf]]
                    wk_ = (qk0_toks["k"] if pp == 0 else toks_A_stores) + [pair_free[buf]]
                    def ql(e, s, pp=pp, buf=buf):
                        return e.dma_start(out=Qt[buf][:], in_=q_scr[pp])
                    p.emit("sync", ql, waits=wq_, inc=(f"xf{buf}", 16))
                    def kl(e, s, pp=pp, buf=buf):
                        return e.dma_start(out=Kt[buf][:], in_=k_scr[pp])
                    p.emit("sync", kl, waits=wk_, inc=(f"xg{buf}", 16))
                    qk_done[buf] = [p.tok(f"xf{buf}"), p.tok(f"xg{buf}")]

                load_vt(0)
                load_qk(0)

                class NormSched:
                    def __init__(self, pp, rls_toks, oc_toks):
                        self.pp = pp
                        self.rls_toks = rls_toks
                        self.oc_toks = oc_toks
                        self.row_toks = [None] * 16
                        self.bbs = [None] * 16
                        self.ji = 0
                    def done(self):
                        return self.ji >= 16
                    def emit_rld(self, idx):
                        h = idx // 8
                        bb = bc_rot[0] % 2
                        bc_rot[0] += 1
                        self.bbs[idx] = bb
                        def rld(e, s, pp=self.pp, idx=idx, bb=bb, h=h):
                            return e.dma_start(
                                out=bc[bb][h * 64:(h + 1) * 64, :],
                                in_=rl_scr[pp, idx * 512:(idx + 1) * 512].partition_broadcast(64))
                        p.emit("gpsimd", rld, waits=[self.rls_toks[idx], bc_free[bb]],
                               inc=(f"rb{bb}", 16))
                        self.row_toks[idx] = p.tok(f"rb{bb}")
                    def emit_one(self):
                        idx = self.ji
                        if self.rls_toks[idx] is None or self.oc_toks[idx] is None:
                            return False
                        if self.row_toks[idx] is None:
                            self.emit_rld(idx)
                        if (idx + 1 < 16 and self.rls_toks[idx + 1] is not None
                                and self.row_toks[idx + 1] is None):
                            self.emit_rld(idx + 1)
                        self.ji += 1
                        pp = self.pp
                        h, sb = divmod(idx, 8)
                        bb = self.bbs[idx]
                        def ntt(e, s, h=h, sb=sb, pp=pp, bb=bb):
                            sl = o_stage[pp % 3][h * 64:(h + 1) * 64, sb * 512:(sb + 1) * 512]
                            return e.tensor_tensor(out=sl, in0=sl,
                                                   in1=bc[bb][h * 64:(h + 1) * 64, :], op=ALU.mult)
                        p.emit("vector", ntt, waits=[self.row_toks[idx], self.oc_toks[idx]],
                               inc=("sV", 1))
                        bc_free[bb] = p.tok("sV")
                        ret = True
                        if self.done():
                            osem = f"os{pp % 3}"
                            tok_n = p.tok("sV")
                            def ost(e, s, pp=pp):
                                return e.dma_start(out=o_scr[pp], in_=o_stage[pp % 3][:])
                            p.emit("gpsimd", ost, waits=[tok_n], inc=(osem, 16))
                            ostage_free[pp % 3] = p.tok(osem)
                        return ret

                deferred_chain = [None]
                for pp in range(NPAIR):
                    buf = pp % 2
                    scheds = []
                    if pp + 1 < NPAIR:
                        load_vt(pp + 1)
                        load_qk(pp + 1)
                    if deferred_chain[0] is not None:
                        nsched = deferred_chain[0]()
                        deferred_chain[0] = None
                        scheds.append(nsched)
                    lw = [vt_tok[buf]] + qk_done[buf]
                    rls_toks = [None] * 16
                    oc_toks = [None] * 16
                    if pp == NPAIR - 1:
                        scheds.append(NormSched(pp, rls_toks, oc_toks))
                    pend = []
                    s_count = 0

                    def emit_pv(job):
                        (sb, h, c0, ncg, gb, pb, etok, vtok, nchunk, is_last) = job
                        bank = (sb * 2 + h) % 2
                        for ci in range(ncg):
                            c = c0 + ci
                            def pvf(e, s, c=c, ci=ci, pb=pb, h=h, buf=buf, nchunk=nchunk, bank=bank):
                                return e.matmul(
                                    out=pPV[bank][0:65, :],
                                    lhsT=Vt[buf][:, c * 130 + h * 65: c * 130 + (h + 1) * 65],
                                    rhs=Pbuf[pb][:, ci * 512:(ci + 1) * 512],
                                    start=(c == 0), stop=(c == nchunk - 1))
                            waits = [etok] + ppv_free[bank]
                            if vtok is not None:
                                waits.append(vtok)
                            p.emit("tensor", pvf, waits=waits,
                                   inc=("sP", 1) if ci == ncg - 1 else None)
                        pbuf_free[pb] = p.tok("sP")
                        if is_last:
                            tok_pv = p.tok("sP")
                            lsb = (sb * 2 + h) % 3
                            def rcp(e, s, lsb=lsb, bank=bank):
                                return e.reciprocal(out=l_stage[lsb][:], in_=pPV[bank][64:65, :])
                            p.emit("vector", rcp, waits=[tok_pv, lstage_free[lsb]], inc=("sV", 1))
                            tok_l = p.tok("sV")
                            def ocp(e, s, h=h, sb=sb, obuf=pp % 3, bank=bank):
                                return e.tensor_copy(
                                    out=o_stage[obuf][h * 64:(h + 1) * 64, sb * 512:(sb + 1) * 512],
                                    in_=pPV[bank][0:64, :])
                            p.emit("vector", ocp, waits=[tok_pv, ostage_free[pp % 3]], inc=("sV", 1))
                            oc_toks[h * 8 + sb] = p.tok("sV")
                            ppv_free[bank] = [oc_toks[h * 8 + sb]]
                            def lst(e, s, pp=pp, h=h, sb=sb, lsb=lsb):
                                return e.dma_start(
                                    out=rl_scr[pp, (h * 8 + sb) * 512:(h * 8 + sb + 1) * 512].unsqueeze(0),
                                    in_=l_stage[lsb][0:1, :])
                            p.emit("gpsimd", lst, waits=[tok_l], inc=(f"ls{lsb}", 16))
                            lstage_free[lsb] = p.tok(f"ls{lsb}")
                            rls_toks[h * 8 + sb] = p.tok(f"ls{lsb}")

                    gidx = 0
                    for sb in range(NMACRO):
                        nchunk = 4 * sb + 4
                        ngroups = (nchunk + 2) // 3
                        for h in range(2):
                            r0, r1 = h * 64, (h + 1) * 64
                            for g in range(ngroups):
                                c0 = g * 3
                                ncg = min(3, nchunk - c0)
                                gb = gidx % 2
                                pb = gidx % 8
                                gidx += 1
                                for ci in range(ncg):
                                    c = c0 + ci
                                    def smf(e, s, c=c, ci=ci, gb=gb, sb=sb, r0=r0, r1=r1, buf=buf):
                                        return e.matmul(
                                            out=pS[gb][:, ci * 512:(ci + 1) * 512],
                                            lhsT=Kt[buf][r0:r1, c * 128:(c + 1) * 128],
                                            rhs=Qt[buf][r0:r1, sb * 512:(sb + 1) * 512],
                                            start=True, stop=True)
                                    p.emit("tensor", smf, waits=lw + [ps_free[gb]],
                                           inc=("sP", 1) if ci == ncg - 1 else None)
                                tok_s = p.tok("sP")
                                def expf(e, s, gb=gb, pb=pb, ncg=ncg):
                                    return e.activation(
                                        out=Pbuf[pb][:, 0:ncg * 512], in_=pS[gb][:, 0:ncg * 512],
                                        func=AF.Exp, bias=c_negC(), scale=c_sqk())
                                p.emit("scalar", expf,
                                       waits=[tok_s, pbuf_free[pb], tok_cmemset], inc=("sC", 1))
                                ps_free[gb] = p.tok("sC")
                                etok = p.tok("sC")
                                vtok = None
                                for ci in range(ncg):
                                    c = c0 + ci
                                    j = c - 4 * sb
                                    if j < 0:
                                        continue
                                    if j > 0:
                                        def msf(e, s, pb=pb, ci=ci, j=j):
                                            return e.memset(Pbuf[pb][:, ci * 512: ci * 512 + j * 128], 0.0)
                                        p.emit("gpsimd", msf, waits=[etok], inc=("sG", 1))
                                    def trif(e, s, pb=pb, ci=ci, j=j):
                                        sl = Pbuf[pb][:, ci * 512 + j * 128: ci * 512 + (j + 1) * 128]
                                        return e.tensor_tensor(out=sl, in0=sl, in1=tri[:], op=ALU.mult)
                                    p.emit("gpsimd", trif, waits=[etok], inc=("sG", 1))
                                    vtok = p.tok("sG")
                                pend.append((sb, h, c0, ncg, gb, pb, etok, vtok, nchunk,
                                             g == ngroups - 1))
                                if len(pend) >= 8:
                                    emit_pv(pend.pop(0))
                                    emit_pv(pend.pop(0))
                                s_count += 1
                                if s_count % 3 == 0:
                                    si = s_count // 3
                                    sc = scheds[si % len(scheds)] if scheds else None
                                    if sc is not None and sc.done():
                                        sc = next((x for x in scheds if not x.done()), None)
                                    if sc is not None and not sc.done():
                                        if not sc.emit_one():
                                            alt = next((x for x in scheds
                                                        if x is not sc and not x.done()), None)
                                            if alt is not None:
                                                alt.emit_one()
                    while pend:
                        emit_pv(pend.pop(0))
                    for sc in scheds:
                        while not sc.done():
                            sc.emit_one()
                    pair_free[buf] = p.tok("sP")

                    def chain(pp=pp, rls_toks=rls_toks, oc_toks=oc_toks):
                        return NormSched(pp, rls_toks, oc_toks)
                    if pp + 1 < NPAIR:
                        deferred_chain[0] = chain

                toks_phase2_stores = [p.tok("os0"), p.tok("os1"), p.tok("os2")]
                tok_phase2_pe = p.tok("sP")
                if debug_taps:
                    p.emit("gpsimd", lambda e, s: e.dma_start(out=taps["t_v"], in_=v_scr),
                           waits=toks_A_stores, inc=("tp", 16))
                    p.emit("gpsimd", lambda e, s: e.dma_start(out=taps["t_o"], in_=o_scr),
                           waits=toks_phase2_stores, inc=("tp", 16))
                    p.emit("gpsimd", lambda e, s: e.dma_start(out=taps["t_l"], in_=rl_scr),
                           waits=toks_phase2_stores, inc=("tp", 16))

        with contextlib.ExitStack() as st3:
            E3 = st3.enter_context
            o_fm = [E3(nc.sbuf_tensor(f"o_fm{i}", [128, 4096], BF16)) for i in range(2)]
            xr_f32 = [E3(nc.sbuf_tensor(f"xre{i}", [128, 1024], F32)) for i in range(2)]
            y1_f32 = [E3(nc.sbuf_tensor(f"y1f{i}", [128, 4096], F32)) for i in range(2)]
            y1_bf = E3(nc.sbuf_tensor("y1b_all", [128, 4096], BF16))
            y1_fm = E3(nc.sbuf_tensor("y1_fm", [128, 8 * 513], BF16))
            dmix = E3(nc.sbuf_tensor("dmix", [128, 4096], BF16))
            xr_bf = E3(nc.sbuf_tensor("xrb", [128, 4096], BF16))
            xk_bf = E3(nc.sbuf_tensor("xkb", [128, 4096], BF16))
            kact = dmix
            ktmp = E3(nc.sbuf_tensor("ktmp", [128, 512], F32))
            junk2 = E3(nc.sbuf_tensor("junk2", [128, 512], F32))
            r_tm = [E3(nc.sbuf_tensor(f"rtm{i}", [128, 1024], F32)) for i in range(2)]
            stats = E3(nc.sbuf_tensor("stats", [128, 56], F32))
            y2st = [E3(nc.sbuf_tensor(f"y2st{i}", [128, 1024], F32)) for i in range(2)]
            pM = [E3(nc.psum_tensor(f"pM{i}", [128, 512], F32)) for i in range(6)]
            pT2 = [E3(nc.psum_tensor(f"pT2_{i}", [128, 1024], BF16)) for i in range(2)]

            ofm_tok = [NOT, NOT]
            ofm_free = [NOT, NOT]
            xf_tok = [NOT, NOT]
            xf_free = [NOT, NOT]
            pm_free = [NOT] * 6
            pt2_free = [NOT, NOT]
            y1b_free = [NOT] * 4
            y2_free = [NOT, NOT]
            rtm_free = [NOT, NOT]
            y1f_free = [NOT, NOT]
            y1fm_mix_tok = NOT
            ffn_pe_done = NOT
            zf_arr = [[NOT] * 4 for _ in range(NMACRO)]
            rstd1_arr = [NOT] * NMACRO
            y1bf_arr = [[NOT] * 4 for _ in range(NMACRO)]
            ln1var_arr = [NOT] * NMACRO
            z2fin = NOT
            y1fm_arr = [NOT] * NMACRO
            xkmix_arr = [NOT] * NMACRO
            xrmix_arr = [NOT] * NMACRO
            kact_done = [NOT] * NMACRO

            def ldo(e, s, m, buf):
                dstv = o_fm[buf][:].rearrange("p (oc t) -> p oc t", t=512)
                return e.dma_start(out=dstv,
                                   in_=o_scr[:, :, m * 512:(m + 1) * 512].transpose([1, 0, 2]))
            p.emit("sync", lambda e, s: ldo(e, s, 0, 0),
                   waits=toks_phase2_stores + [tok_phase2_pe], inc=("of0", 16))
            ofm_tok[0] = p.tok("of0")

            def mk_ln_helpers(base):
                def lns_sum(e, s):
                    return e.tensor_add(out=stats[:, base + 16:base + 20],
                                        in0=stats[:, base:base + 4], in1=stats[:, base + 4:base + 8])
                def lns_meansc(e, s):
                    return e.tensor_scalar(out=stats[:, base + 16:base + 20], in0=stats[:, base + 16:base + 20],
                                           scalar1=1.0 / 1024, scalar2=None, op0=ALU.mult)
                def lns_sumq(e, s):
                    return e.tensor_add(out=stats[:, base + 20:base + 24],
                                        in0=stats[:, base + 8:base + 12], in1=stats[:, base + 12:base + 16])
                def lns_ex2sc(e, s):
                    return e.tensor_scalar(out=stats[:, base + 20:base + 24], in0=stats[:, base + 20:base + 24],
                                           scalar1=1.0 / 1024, scalar2=None, op0=ALU.mult)
                def lns_m2(e, s):
                    return e.tensor_mul(out=stats[:, base + 24:base + 28],
                                        in0=stats[:, base + 16:base + 20], in1=stats[:, base + 16:base + 20])
                def lns_var(e, s):
                    return e.tensor_sub(out=stats[:, base + 20:base + 24],
                                        in0=stats[:, base + 20:base + 24], in1=stats[:, base + 24:base + 28])
                def lns_ln(e, s):
                    return e.activation(out=stats[:, base + 20:base + 24], in_=stats[:, base + 20:base + 24],
                                        func=AF.Ln, bias=c_eps())
                def lns_exp(e, s):
                    return e.activation(out=stats[:, base + 20:base + 24], in_=stats[:, base + 20:base + 24],
                                        func=AF.Exp, scale=c_nh())
                return [lns_sum, lns_meansc, lns_sumq, lns_ex2sc, lns_m2, lns_var, lns_ln, lns_exp]

            LN1H = mk_ln_helpers(0)
            LN2H = mk_ln_helpers(28)

            def emit_ln_dve(H, first_wait_tok):
                def vs():
                    t = p.tok("sV")
                    p.emit("vector", lambda e, s, t=t: e.wait_ge(s[t[0]], t[1]))
                p.emit("vector", lambda e, s, t=first_wait_tok: e.wait_ge(s[t[0]], t[1]))
                p.emit("vector", H[0], inc=("sV", 1))
                vs()
                p.emit("vector", H[1], inc=("sV", 1))
                p.emit("vector", H[2], inc=("sV", 1))
                vs()
                p.emit("vector", H[3], inc=("sV", 1))
                p.emit("vector", H[4], inc=("sV", 1))
                vs()
                p.emit("vector", H[5], inc=("sV", 1))
                return p.tok("sV")

            def emit_ln_act(H, tok_var):
                p.emit("scalar", H[6], waits=[tok_var], inc=("sC", 1))
                tln = p.tok("sC")
                p.emit("scalar", lambda e, s, t=tln: e.wait_ge(s[t[0]], t[1]))
                p.emit("scalar", H[7], inc=("sC", 1))
                return p.tok("sC")

            def emit_ln_stats(H, first_wait_tok):
                return emit_ln_act(H, emit_ln_dve(H, first_wait_tok))

            def emit_proj(m):
                buf = m % 2
                if m + 1 < NMACRO:
                    nb = 1 - buf
                    p.emit("sync", lambda e, s, m=m, nb=nb: ldo(e, s, m + 1, nb),
                           waits=toks_phase2_stores + [ofm_free[nb]], inc=(f"of{nb}", 16))
                    ofm_tok[nb] = p.tok(f"of{nb}")
                for ts in range(4):
                    tsb = ts % 2
                    def xre(e, s, m=m, ts=ts, tsb=tsb):
                        return e.dma_start(out=xr_f32[tsb][:],
                                           in_=x_d[m * 512 + ts * 128: m * 512 + (ts + 1) * 128, :])
                    p.emit("sync", xre, waits=[xf_free[tsb], tok_phase2_pe], inc=(f"xg{tsb}", 16))
                    xf_tok[tsb] = p.tok(f"xg{tsb}")
                    for half in range(2):
                        bi = (ts * 2 + half) % 3
                        bank = pM[bi]
                        for oc in range(8):
                            def amf(e, s, oc=oc, half=half, ts=ts, bank=bank, buf=buf):
                                return e.matmul(
                                    out=bank[:],
                                    lhsT=o_fm[buf][:, oc * 512 + ts * 128: oc * 512 + (ts + 1) * 128],
                                    rhs=wT["wp"][:, oc * 1024 + half * 512: oc * 1024 + (half + 1) * 512],
                                    start=(oc == 0), stop=(oc == 7))
                            p.emit("tensor", amf, waits=[ofm_tok[buf], pm_free[bi]],
                                   inc=("sP", 1) if oc == 7 else None)
                        tok_mm = p.tok("sP")
                        def zf(e, s, bank=bank, half=half, ts=ts, tsb=tsb, buf=buf):
                            return e.scalar_tensor_tensor(
                                out=y1_f32[buf][:, ts * 1024 + half * 512: ts * 1024 + (half + 1) * 512],
                                in0=bank[:], scalar=SPV,
                                in1=xr_f32[tsb][:, half * 512:(half + 1) * 512],
                                op0=ALU.mult, op1=ALU.add,
                                accum_out=stats[:, half * 4 + ts: half * 4 + ts + 1])
                        p.emit("vector", zf, waits=[tok_mm, xf_tok[tsb], y1f_free[buf]],
                               inc=("sV", 1))
                        pm_free[bi] = p.tok("sV")
                    zf_arr[m][ts] = p.tok("sV")
                    xf_free[tsb] = zf_arr[m][ts]
                ofm_free[buf] = p.tok("sP")

            def emit_tailB1(m):
                p.emit("vector", lambda e, s, t=y1fm_arr[m]: e.wait_ge(s[t[0]], t[1]))
                def dmf(e, s):
                    v1 = y1_fm[:, :].rearrange("p (cc x) -> p cc x", x=513)[:, :, 1:513]
                    v0 = y1_fm[:, :].rearrange("p (cc x) -> p cc x", x=513)[:, :, 0:512]
                    return e.tensor_sub(out=dmix[:].rearrange("p (cc x) -> p cc x", x=512),
                                        in0=v1, in1=v0)
                p.emit("vector", dmf, waits=[ffn_pe_done], inc=("sV", 1))
                for cc in range(8):
                    def mixf(e, s, cc=cc):
                        return e.scalar_tensor_tensor(
                            out=xk_bf[:, cc * 512:(cc + 1) * 512],
                            in0=dmix[:, cc * 512:(cc + 1) * 512], scalar=muk_fm[:, cc:cc + 1],
                            in1=y1_fm[:, cc * 513: cc * 513 + 512],
                            op0=ALU.mult, op1=ALU.add)
                    p.emit("vector", mixf, inc=("sV", 1) if cc == 7 else None)
                xkmix_arr[m] = p.tok("sV")

            def emit_tailB2(m):
                nonlocal y1fm_mix_tok
                for cc in range(8):
                    def mixf(e, s, cc=cc):
                        return e.scalar_tensor_tensor(
                            out=xr_bf[:, cc * 512:(cc + 1) * 512],
                            in0=dmix[:, cc * 512:(cc + 1) * 512], scalar=mur_fm[:, cc:cc + 1],
                            in1=y1_fm[:, cc * 513: cc * 513 + 512],
                            op0=ALU.mult, op1=ALU.add)
                    p.emit("vector", mixf, inc=("sV", 1) if cc == 7 else None)
                xrmix_arr[m] = p.tok("sV")
                def bnd(e, s):
                    v = y1_fm[:, :].rearrange("p (cc x) -> p cc x", x=513)
                    return e.tensor_copy(out=v[:, :, 0:1], in_=v[:, :, 512:513])
                p.emit("vector", bnd, inc=("sV", 1))
                y1fm_mix_tok = p.tok("sV")

            def emit_tailA1(m):
                buf = m % 2
                for ts in range(4):
                    for half in range(2):
                        def zq(e, s, half=half, ts=ts, buf=buf):
                            sl = y1_f32[buf][:, ts * 1024 + half * 512: ts * 1024 + (half + 1) * 512]
                            return e.scalar_tensor_tensor(
                                out=junk2[:], in0=sl, scalar=1.0, in1=sl,
                                op0=ALU.mult, op1=ALU.mult,
                                accum_out=stats[:, 8 + half * 4 + ts: 9 + half * 4 + ts])
                        p.emit("vector", zq,
                               waits=[zf_arr[m][ts]] if half == 0 else [], inc=("sV", 1))
                var_tok = emit_ln_dve(LN1H, p.tok("sV"))
                ln1var_arr[m] = var_tok
                if m == 0:
                    p.emit("vector", lambda e, s: e.memset(
                        y1_fm[:, :].rearrange("p (cc x) -> p cc x", x=513)[:, :, 0:1], 0.0),
                        inc=("sV", 1))
                    y1fm_arr[0] = p.tok("sV")

            def emit_ln1_finish(m):
                buf = m % 2
                tok_rstd1 = emit_ln_act(LN1H, ln1var_arr[m])
                rstd1_arr[m] = tok_rstd1
                for ts in range(4):
                    def y1f2(e, s, ts=ts, buf=buf):
                        sl = y1_f32[buf][:, ts * 1024:(ts + 1) * 1024]
                        return e.tensor_scalar(
                            out=sl, in0=sl, scalar1=stats[:, 16 + ts:17 + ts],
                            scalar2=stats[:, 20 + ts:21 + ts],
                            op0=ALU.subtract, op1=ALU.mult)
                    p.emit("vector", y1f2, waits=[tok_rstd1], inc=("sV", 1))
                    def y1b(e, s, ts=ts, buf=buf):
                        return e.tensor_copy(out=y1_bf[:, ts * 1024:(ts + 1) * 1024],
                                             in_=y1_f32[buf][:, ts * 1024:(ts + 1) * 1024])
                    p.emit("vector", y1b, waits=[y1b_free[ts]], inc=("sV", 1))
                    y1bf_arr[m][ts] = p.tok("sV")

            def emit_tailA2(m):
                for ts in range(4):
                    tsb = ts % 2
                    bankT = pT2[ts % 2]
                    for cc in range(8):
                        def trf2(e, s, cc=cc, bankT=bankT, ts=ts):
                            return e.transpose(out=bankT[:, cc * 128:(cc + 1) * 128],
                                               in_=y1_bf[:, ts * 1024 + cc * 128: ts * 1024 + (cc + 1) * 128],
                                               identity=identb[:])
                        p.emit("tensor", trf2,
                               waits=[y1bf_arr[m][ts], pt2_free[ts % 2], y1fm_mix_tok],
                               inc=("sP", 1) if cc == 7 else None)
                    tok_tr = p.tok("sP")
                    y1b_free[ts] = tok_tr
                    def evy(e, s, ts=ts, bankT=bankT):
                        dv = y1_fm[:, :].rearrange("p (cc x) -> p cc x", x=513)[:, :, 1 + ts * 128: 1 + (ts + 1) * 128]
                        return e.copy(out=dv, in_=bankT[:].rearrange("p (cc x) -> p cc x", x=128))
                    p.emit("scalar", evy, waits=[tok_tr], inc=("sC", 1))
                    pt2_free[ts % 2] = p.tok("sC")
                y1fm_arr[m] = p.tok("sC")

            def emit_stage2(m, mnext):
                nonlocal ffn_pe_done, z2fin
                buf = m % 2
                for dc in range(8):
                    bi = 2 + dc % 2
                    bank = pM[bi]
                    for cc in range(8):
                        def kmf(e, s, cc=cc, dc=dc, bank=bank):
                            return e.matmul(
                                out=bank[:],
                                lhsT=wT["wk2"][:, cc * 1024 + dc * 128: cc * 1024 + (dc + 1) * 128],
                                rhs=xk_bf[:, cc * 512:(cc + 1) * 512],
                                start=(cc == 0), stop=(cc == 7))
                        p.emit("tensor", kmf, waits=[xkmix_arr[m], pm_free[bi]],
                               inc=("sP", 1) if cc == 7 else None)
                    tok_mm = p.tok("sP")
                    def rlf(e, s, bank=bank):
                        return e.activation(out=ktmp[:], in_=bank[:], func=AF.Relu)
                    p.emit("scalar", rlf, waits=[tok_mm, z2fin], inc=("sC", 1))
                    pm_free[bi] = p.tok("sC")
                    trl = p.tok("sC")
                    p.emit("scalar", lambda e, s, t=trl: e.wait_ge(s[t[0]], t[1]))
                    def sqf(e, s, dc=dc):
                        return e.activation(out=kact[:, dc * 512:(dc + 1) * 512],
                                            in_=ktmp[:], func=AF.Square)
                    p.emit("scalar", sqf, inc=("sC", 1))
                tok_kact = p.tok("sC")
                kact_done[m] = tok_kact
                if mnext is not None:
                    emit_ln1_finish(mnext)
                z2_toks = [NOT] * 4
                r_tok = [NOT, NOT]
                for ts in range(4):
                    tsb = ts % 2
                    for half in range(2):
                        bi = 4 + half
                        bank = pM[bi]
                        for cc in range(8):
                            def rmf(e, s, cc=cc, ts=ts, half=half, bank=bank):
                                return e.matmul(
                                    out=bank[:],
                                    lhsT=xr_bf[:, cc * 512 + ts * 128: cc * 512 + (ts + 1) * 128],
                                    rhs=wT["wr"][:, cc * 1024 + half * 512: cc * 1024 + (half + 1) * 512],
                                    start=(cc == 0), stop=(cc == 7))
                            p.emit("tensor", rmf, waits=[xrmix_arr[m], pm_free[bi]],
                                   inc=("sP", 1) if cc == 7 else None)
                        tok_mm = p.tok("sP")
                        def sgf(e, s, bank=bank, half=half, tsb=tsb):
                            return e.activation(out=r_tm[tsb][:, half * 512:(half + 1) * 512],
                                                in_=bank[:], func=AF.Sigmoid)
                        p.emit("scalar", sgf, waits=[tok_mm, rtm_free[tsb]], inc=("sC", 1))
                        pm_free[bi] = p.tok("sC")
                    r_tok[tsb] = p.tok("sC")
                    for half in range(2):
                        bi = 2 + half
                        bank = pM[bi]
                        for dc in range(8):
                            def vmf2(e, s, dc=dc, ts=ts, half=half, bank=bank):
                                return e.matmul(
                                    out=bank[:],
                                    lhsT=kact[:, dc * 512 + ts * 128: dc * 512 + (ts + 1) * 128],
                                    rhs=wT["wv2"][:, dc * 1024 + half * 512: dc * 1024 + (half + 1) * 512],
                                    start=(dc == 0), stop=(dc == 7))
                            p.emit("tensor", vmf2, waits=[tok_kact, pm_free[bi]],
                                   inc=("sP", 1) if dc == 7 else None)
                        tok_mm = p.tok("sP")
                        def ff(e, s, bank=bank, half=half, tsb=tsb):
                            return e.tensor_mul(out=ktmp[:], in0=bank[:],
                                                in1=r_tm[tsb][:, half * 512:(half + 1) * 512])
                        p.emit("vector", ff, waits=[tok_mm, r_tok[tsb]], inc=("sV", 1))
                        pm_free[bi] = p.tok("sV")
                        def z2f(e, s, half=half, ts=ts, buf=buf):
                            sl = y1_f32[buf][:, ts * 1024 + half * 512: ts * 1024 + (half + 1) * 512]
                            return e.scalar_tensor_tensor(
                                out=sl, in0=ktmp[:], scalar=1.0, in1=sl,
                                op0=ALU.mult, op1=ALU.add,
                                accum_out=stats[:, 28 + half * 4 + ts: 29 + half * 4 + ts])
                        p.emit("vector", z2f, inc=("sV", 1))
                        def z2q(e, s, half=half, ts=ts, buf=buf):
                            sl = y1_f32[buf][:, ts * 1024 + half * 512: ts * 1024 + (half + 1) * 512]
                            return e.scalar_tensor_tensor(
                                out=junk2[:], in0=sl, scalar=1.0, in1=sl,
                                op0=ALU.mult, op1=ALU.mult,
                                accum_out=stats[:, 36 + half * 4 + ts: 37 + half * 4 + ts])
                        p.emit("vector", z2q, inc=("sV", 1))
                        z2_toks[ts] = p.tok("sV")
                    rtm_free[tsb] = z2_toks[ts]
                tok_rstd2 = emit_ln_stats(LN2H, z2_toks[3])
                for ts in range(4):
                    tsb = ts % 2
                    def y2f(e, s, ts=ts, tsb=tsb, buf=buf):
                        return e.tensor_scalar(
                            out=y2st[tsb][:],
                            in0=y1_f32[buf][:, ts * 1024:(ts + 1) * 1024],
                            scalar1=stats[:, 44 + ts:45 + ts],
                            scalar2=stats[:, 48 + ts:49 + ts],
                            op0=ALU.subtract, op1=ALU.mult)
                    p.emit("vector", y2f, waits=[tok_rstd2, y2_free[tsb]], inc=("sV", 1))
                    tok_y2 = p.tok("sV")
                    def yst(e, s, m=m, ts=ts, tsb=tsb):
                        return e.dma_start(out=y_d[m * 512 + ts * 128: m * 512 + (ts + 1) * 128, :],
                                           in_=y2st[tsb][:])
                    p.emit("gpsimd", yst, waits=[tok_y2], inc=(f"ys{tsb}", 16))
                    y2_free[tsb] = p.tok(f"ys{tsb}")
                y1f_free[buf] = p.tok("sV")
                ffn_pe_done = p.tok("sP")
                z2fin = z2_toks[3]

            for m in range(NMACRO + 1):
                if m >= 1:
                    emit_tailB1(m - 1)
                if m < NMACRO:
                    emit_proj(m)
                if m >= 1:
                    emit_tailB2(m - 1)
                if m < NMACRO:
                    emit_tailA1(m)
                if m == 0:
                    emit_ln1_finish(0)
                if m >= 1:
                    emit_stage2(m - 1, m if m < NMACRO else None)
                if m < NMACRO:
                    emit_tailA2(m)

            finals = [p.tok(sm) for sm in DMA_SEMS if p.cnt[sm] > 0]
            def final_wait(e, s, finals=finals):
                for sm, v in finals:
                    e.wait_ge(s[sm], v)
            p.emit("sync", final_wait)

        with contextlib.ExitStack() as stb:
            sems = {nm: stb.enter_context(nc.semaphore(nm)) for nm in p.sem_names()}
            block = stb.enter_context(nc.Block())
            block.sync(lambda e: p.run("sync", e, sems))
            block.tensor(lambda e: p.run("tensor", e, sems))
            block.vector(lambda e: p.run("vector", e, sems))
            block.scalar(lambda e: p.run("scalar", e, sems))
            block.gpsimd(lambda e: p.run("gpsimd", e, sems))
    return nc


def kernel(L0_attn_x, L0_attn_W_key, L0_attn_W_query, L0_attn_W_value, L0_attn_W_proj,
           L0_ffn_W_r, L0_ffn_W_k, L0_ffn_W_v, L0_ffn_mu_r, L0_ffn_mu_k):
    x = np.ascontiguousarray(np.asarray(L0_attn_x, dtype=np.float32))
    ws = {
        "wq": np.ascontiguousarray(np.asarray(L0_attn_W_query, np.float32)),
        "wk": np.ascontiguousarray(np.asarray(L0_attn_W_key, np.float32)),
        "wv": np.ascontiguousarray(np.asarray(L0_attn_W_value, np.float32)),
        "wp": np.ascontiguousarray(np.asarray(L0_attn_W_proj, np.float32)),
        "wr": np.ascontiguousarray(np.asarray(L0_ffn_W_r, np.float32)),
        "wk2": np.ascontiguousarray(np.asarray(L0_ffn_W_k, np.float32)),
        "wv2": np.ascontiguousarray(np.asarray(L0_ffn_W_v, np.float32)),
    }
    mu_r = np.ascontiguousarray(np.asarray(L0_ffn_mu_r, np.float32))
    mu_k = np.ascontiguousarray(np.asarray(L0_ffn_mu_k, np.float32))
    identb = np.eye(128, dtype=ml_dtypes.bfloat16)
    identf = np.eye(128, dtype=np.float32)
    tri = np.triu(np.ones((128, 128), dtype=ml_dtypes.bfloat16))

    nc = build_program()
    in_maps = []
    for b in range(8):
        mm = {"x": x[b], "mu_r": mu_r, "mu_k": mu_k,
              "identb": identb, "identf": identf, "tri": tri}
        mm.update(ws)
        in_maps.append(mm)
    res = run_bass_kernel_spmd(nc, in_maps, core_ids=list(range(8)))
    return np.stack([res.results[b]["y"] for b in range(8)], axis=0)


# version 32
# speedup vs baseline: 1.1684x; 1.1684x over previous
import contextlib
import numpy as np
import ml_dtypes
import concourse.bass as bass
import concourse.mybir as mybir
from concourse.bass_utils import run_bass_kernel_spmd

F32 = mybir.dt.float32
BF16 = mybir.dt.bfloat16
AF = mybir.ActivationFunctionType
ALU = mybir.AluOpType

T = 4096
C = 1024
NMACRO = 8
NPAIR = 8
NEGC = -30.0
SQK = 1.0 / 8192.0
SPV = 1.0 / 1024.0

ENGS = ("sync", "tensor", "vector", "scalar", "gpsimd")

DMA_SEMS = (["mA", "wL", "lre", "rls", "tp", "os2"]
            + [f"{nm}{i}" for nm in ("xL", "pr", "bc", "ls", "os", "of", "xg", "ys", "xs", "xf", "rb")
               for i in range(2)]
            + [f"sg{i}" for i in range(4)] + ["ls2", "xs2", "xs3"])
COMPUTE_SEMS = ["sP", "sV", "sC", "sG"]


class Prog:

    def __init__(self):
        self.ops = {e: [] for e in ENGS}
        self.cnt = {}
        self.guar = {e: {} for e in ENGS}

    def add_sem(self, name):
        self.cnt[name] = 0

    def sem_names(self):
        return list(self.cnt.keys())

    def emit(self, eng, fn, waits=(), inc=None):
        w = []
        for sem, val in waits:
            if val <= 0:
                continue
            assert val <= self.cnt[sem], f"wait on future token {sem}={val}>{self.cnt[sem]}"
            if self.guar[eng].get(sem, 0) >= val:
                continue
            self.guar[eng][sem] = val
            w.append((sem, val))
        if inc is not None:
            sem, n = inc
            self.cnt[sem] += n
            if n < 16 and eng != "sync":
                self.guar[eng][sem] = max(self.guar[eng].get(sem, 0), self.cnt[sem])
        self.ops[eng].append((fn, w, inc))

    def tok(self, sem):
        return (sem, self.cnt[sem])

    def run(self, eng_name, eng, sems):
        for fn, waits, inc in self.ops[eng_name]:
            for sem, val in waits:
                eng.wait_ge(sems[sem], val)
            ins = fn(eng, sems)
            if inc is not None:
                assert ins is not None, f"inc {inc} on op returning None ({eng_name})"
                ins.then_inc(sems[inc[0]], inc[1])


def build_program(debug_taps=False):
    nc = bass.Bass("TRN2", target_bir_lowering=False, debug=False)

    x_d = nc.dram_tensor("x", [T, C], F32, kind="ExternalInput").ap()
    w_d = {}
    for nm in ["wq", "wk", "wv", "wp", "wr", "wk2", "wv2"]:
        w_d[nm] = nc.dram_tensor(nm, [C, C], F32, kind="ExternalInput").ap()
    mur_d = nc.dram_tensor("mu_r", [C], F32, kind="ExternalInput").ap()
    muk_d = nc.dram_tensor("mu_k", [C], F32, kind="ExternalInput").ap()
    identb_d = nc.dram_tensor("identb", [128, 128], BF16, kind="ExternalInput").ap()
    identf_d = nc.dram_tensor("identf", [128, 128], F32, kind="ExternalInput").ap()
    tri_d = nc.dram_tensor("tri", [128, 128], BF16, kind="ExternalInput").ap()
    y_d = nc.dram_tensor("y", [T, C], F32, kind="ExternalOutput").ap()
    v_scr = nc.dram_tensor("v_scr", [NPAIR, 2, 32, 128, 64], BF16, kind="Internal").ap()
    q_scr = nc.dram_tensor("q_scr", [NPAIR, 128, 4096], BF16, kind="Internal").ap()
    k_scr = nc.dram_tensor("k_scr", [NPAIR, 128, 4096], BF16, kind="Internal").ap()
    o_scr = nc.dram_tensor("o_scr", [NPAIR, 128, T], BF16, kind="Internal").ap()
    rl_scr = nc.dram_tensor("rl_scr", [NPAIR, 8192], F32, kind="Internal").ap()
    taps = {}
    if debug_taps:
        taps["t_v"] = nc.dram_tensor("t_v", [NPAIR, 2, 32, 128, 64], BF16, kind="ExternalOutput").ap()
        taps["t_o"] = nc.dram_tensor("t_o", [NPAIR, 128, T], BF16, kind="ExternalOutput").ap()
        taps["t_l"] = nc.dram_tensor("t_l", [NPAIR, 8192], F32, kind="ExternalOutput").ap()
        taps["t_z"] = nc.dram_tensor("t_z", [128, 1024], F32, kind="ExternalOutput").ap()
        taps["t_att"] = nc.dram_tensor("t_att", [128, 1024], F32, kind="ExternalOutput").ap()
        taps["t_xre"] = nc.dram_tensor("t_xre", [128, 1024], F32, kind="ExternalOutput").ap()
        taps["t_stats"] = nc.dram_tensor("t_stats", [128, 28], F32, kind="ExternalOutput").ap()
        taps["t_y1"] = nc.dram_tensor("t_y1", [128, 4096], F32, kind="ExternalOutput").ap()
        taps["t_xr"] = nc.dram_tensor("t_xr", [128, 4096], BF16, kind="ExternalOutput").ap()
        taps["t_kact"] = nc.dram_tensor("t_kact", [128, 4096], BF16, kind="ExternalOutput").ap()
        taps["t_r"] = nc.dram_tensor("t_r", [128, 1024], F32, kind="ExternalOutput").ap()
        taps["t_z2"] = nc.dram_tensor("t_z2", [128, 1024], F32, kind="ExternalOutput").ap()

    p = Prog()
    for s in DMA_SEMS + COMPUTE_SEMS:
        p.add_sem(s)
    NOT = ("sP", 0)

    def tap_barrier():
        tokb = p.tok("tp")
        for en in ENGS:
            p.emit(en, lambda e, s, tokb=tokb: e.wait_ge(s[tokb[0]], tokb[1]))

    with contextlib.ExitStack() as st0:
        E0 = st0.enter_context
        identb = E0(nc.sbuf_tensor("s_identb", [128, 128], BF16))
        identf = E0(nc.sbuf_tensor("s_identf", [128, 128], F32))
        tri = E0(nc.sbuf_tensor("s_tri", [128, 128], BF16))
        mur_fm = E0(nc.sbuf_tensor("mur_fm", [128, 8], F32))
        muk_fm = E0(nc.sbuf_tensor("muk_fm", [128, 8], F32))
        consts = E0(nc.sbuf_tensor("consts", [128, 4], F32))
        wT = {}
        for nm in ["wp", "wr", "wk2", "wv2"]:
            wT[nm] = E0(nc.sbuf_tensor("T_" + nm, [128, 8192], BF16))

        def c_negC(): return consts[:, 0:1]
        def c_sqk(): return consts[:, 1:2]
        def c_eps(): return consts[:, 2:3]
        def c_nh(): return consts[:, 3:4]

        p.emit("sync", lambda e, s: e.dma_start(out=identb[:], in_=identb_d), inc=("mA", 16))
        p.emit("sync", lambda e, s: e.dma_start(out=identf[:], in_=identf_d), inc=("mA", 16))
        p.emit("sync", lambda e, s: e.dma_start(out=tri[:], in_=tri_d), inc=("mA", 16))
        tok_consts = p.tok("mA")
        p.emit("vector", lambda e, s: e.memset(consts[:, 0:1], NEGC))
        p.emit("vector", lambda e, s: e.memset(consts[:, 1:2], SQK))
        p.emit("vector", lambda e, s: e.memset(consts[:, 2:3], 1e-5))
        p.emit("vector", lambda e, s: e.memset(consts[:, 3:4], -0.5), inc=("sV", 1))
        tok_cmemset = p.tok("sV")

        with contextlib.ExitStack() as stAB:
            EAB = stAB.enter_context
            wTq = EAB(nc.sbuf_tensor("T_wq", [128, 8192], BF16))
            wTk = EAB(nc.sbuf_tensor("T_wk", [128, 8192], BF16))
            o_stage = [EAB(nc.sbuf_tensor(f"o_stage{i}", [128, 4096], BF16)) for i in range(3)]
            l_stage = [EAB(nc.sbuf_tensor(f"l_stage{i}", [1, 512], F32)) for i in range(3)]

            with contextlib.ExitStack() as st1:
                E1 = st1.enter_context
                wnat = E1(nc.sbuf_tensor("wnat", [128, 8192], BF16))
                wTv = E1(nc.sbuf_tensor("T_wv", [128, 8192], BF16))
                x_bf = [E1(nc.sbuf_tensor("x_bf0", [128, 4096], BF16))] * 2
                x_fm = [E1(nc.sbuf_tensor(f"x_fm{i}", [128, 4096], BF16)) for i in range(2)]
                mu8 = E1(nc.sbuf_tensor("mu8", [8, 128], F32))
                stg = [E1(nc.sbuf_tensor(f"stg{i}", [128, 512], BF16)) for i in range(4)]
                qks = [E1(nc.sbuf_tensor(f"qks{i}", [128, 512], BF16)) for i in range(4)]
                pV = [E1(nc.psum_tensor(f"pV{i}", [128, 512], F32)) for i in range(3)]
                pT = [E1(nc.psum_tensor(f"pT{i}", [128, 1024], BF16)) for i in range(2)]
                pQK = [E1(nc.psum_tensor(f"pQK{i}", [128, 512], F32)) for i in range(3)]
                pF = pT[0][:, 0:16].bitcast(F32)

                p.emit("sync", lambda e, s: e.dma_start(
                    out=mu8[:], in_=mur_d.rearrange("(c p) -> c p", p=128)), inc=("mA", 16))
                tok = p.tok("mA")
                p.emit("tensor", lambda e, s: e.transpose(out=pF, in_=mu8[:], identity=identf[0:8, 0:8]),
                       waits=[tok, tok_consts], inc=("sP", 1))
                tokp = p.tok("sP")
                p.emit("vector", lambda e, s: e.tensor_copy(out=mur_fm[:], in_=pF),
                       waits=[tokp], inc=("sV", 1))
                tokv = p.tok("sV")
                p.emit("sync", lambda e, s: e.dma_start(
                    out=mu8[:], in_=muk_d.rearrange("(c p) -> c p", p=128)),
                    waits=[tokp], inc=("mA", 16))
                tok2 = p.tok("mA")
                p.emit("tensor", lambda e, s: e.transpose(out=pF, in_=mu8[:], identity=identf[0:8, 0:8]),
                       waits=[tok2, tokv], inc=("sP", 1))
                tokp2 = p.tok("sP")
                p.emit("vector", lambda e, s: e.tensor_copy(out=muk_fm[:], in_=pF),
                       waits=[tokp2], inc=("sV", 1))
                tok_mu_evac = p.tok("sV")

                evac_tok = [tok_mu_evac, NOT]
                wnat_free = NOT
                wload_tok = NOT
                wready = {}
                wdsts = {"wq": wTq, "wk": wTk, "wv": wTv, "wp": wT["wp"], "wr": wT["wr"],
                         "wk2": wT["wk2"], "wv2": wT["wv2"]}

                def wprep_load(nm):
                    nonlocal wload_tok
                    wdram = w_d[nm]
                    for hf in range(2):
                        def wload(e, s, wdram=wdram, hf=hf):
                            return e.dma_start(
                                out=wnat[:, hf * 4096:(hf + 1) * 4096].rearrange(
                                    "p (oc c) -> p oc c", c=1024),
                                in_=wdram[hf * 512:(hf + 1) * 512, :].rearrange(
                                    "(oc p) c -> oc p c", p=128).transpose([1, 0, 2]))
                        p.emit("gpsimd", wload, waits=[wnat_free], inc=("wL", 16))
                    wload_tok = p.tok("wL")

                def wprep_tr(nm):
                    nonlocal wnat_free
                    dst = wdsts[nm]
                    for oc in range(8):
                        bank = pT[oc % 2]
                        for cc in range(8):
                            def mmf(e, s, oc=oc, cc=cc, bank=bank):
                                return e.transpose(out=bank[:, cc * 128:(cc + 1) * 128],
                                                   in_=wnat[:, oc * 1024 + cc * 128: oc * 1024 + (cc + 1) * 128],
                                                   identity=identb[:])
                            p.emit("tensor", mmf,
                                   waits=[tok_consts, wload_tok, evac_tok[oc % 2]],
                                   inc=("sP", 1) if cc == 7 else None)
                        tok_tr = p.tok("sP")
                        def evacf(e, s, dst=dst, oc=oc, bank=bank):
                            dv = dst[:, :].rearrange("p (cc x) -> p cc x", x=1024)[:, :, oc * 128:(oc + 1) * 128]
                            return e.tensor_copy(out=dv, in_=bank[:].rearrange("p (cc x) -> p cc x", x=128))
                        p.emit("vector", evacf, waits=[tok_tr], inc=("sV", 1))
                        evac_tok[oc % 2] = p.tok("sV")
                    wnat_free = p.tok("sP")
                    wready[nm] = p.tok("sV")

                WORDER = ["wv", "wq", "wk", "wp", "wr", "wk2", "wv2"]
                wprep_load("wv")
                wprep_tr("wv")

                xcast_tok = [[NOT, NOT], [NOT, NOT]]
                xfm_vdone = [NOT, NOT]
                xbf_trdone = [NOT, NOT]
                stg_tok = [NOT] * 4
                pv_tok = [NOT] * 3
                xfm_ready_arr = [NOT, NOT]
                qkrot = [0]
                qk0_toks = {}
                toks_V_stores = []
                pqk_free = [NOT] * 3
                qks_free = [NOT] * 4

                def emit_qk_all(kind, j):
                    wtok = wready["wq" if kind == "q" else "wk"]
                    wTx = wTq if kind == "q" else wTk
                    scr = q_scr if kind == "q" else k_scr
                    xb = j % 2
                    xtok = xfm_ready_arr[xb]
                    for pp in range(NPAIR):
                        bi = qkrot[0] % 3
                        si = qkrot[0] % 4
                        qkrot[0] += 1
                        for cc in range(8):
                            def qmf(e, s, cc=cc, pp=pp, wTx=wTx, xb=xb, bi=bi):
                                return e.matmul(
                                    out=pQK[bi][:],
                                    lhsT=wTx[:, cc * 1024 + pp * 128: cc * 1024 + (pp + 1) * 128],
                                    rhs=x_fm[xb][:, cc * 512:(cc + 1) * 512],
                                    start=(cc == 0), stop=(cc == 7))
                            p.emit("tensor", qmf, waits=[xtok, wtok, pqk_free[bi]],
                                   inc=("sP", 1) if cc == 7 else None)
                        tok_mm = p.tok("sP")
                        def qev(e, s, bi=bi, si=si):
                            return e.copy(out=qks[si][:], in_=pQK[bi][:])
                        p.emit("scalar", qev, waits=[tok_mm, qks_free[si]], inc=("sC", 1))
                        tok_ev = p.tok("sC")
                        pqk_free[bi] = tok_ev
                        def qst(e, s, scr=scr, pp=pp, j=j, si=si):
                            return e.dma_start(out=scr[pp][:, j * 512:(j + 1) * 512],
                                               in_=qks[si][:])
                        p.emit("scalar", qst, waits=[tok_ev], inc=(f"xs{si}", 16))
                        qks_free[si] = p.tok(f"xs{si}")
                        if pp == 0 and j == NMACRO - 1:
                            qk0_toks[kind] = [p.tok(f"xs{i}") for i in range(4)]
                    xfm_vdone[xb] = p.tok("sP")

                for m in range(NMACRO):
                    buf = m % 2
                    if m == 2:
                        emit_qk_all("k", 0)
                    elif m == 3:
                        emit_qk_all("k", 1)
                    xl_sems = (f"xL{buf}", f"bc{buf}")
                    cast_tok_h = [None, None]
                    for hf in range(2):
                        def xload(e, s, m=m, buf=buf, hf=hf):
                            return e.dma_start(
                                out=x_bf[buf][:, hf * 2048:(hf + 1) * 2048].rearrange(
                                    "p (ts c) -> p ts c", c=1024),
                                in_=x_d[m * 512 + hf * 256: m * 512 + (hf + 1) * 256, :].rearrange(
                                    "(ts p) c -> ts p c", p=128).transpose([1, 0, 2]))
                        p.emit("gpsimd", xload, waits=[xbf_trdone[1 - buf]],
                               inc=(xl_sems[hf], 16))
                        cast_tok_h[hf] = p.tok(xl_sems[hf])
                    xcast_tok[buf] = list(cast_tok_h)
                    if m + 1 < len(WORDER):
                        wprep_load(WORDER[m + 1])
                    for ts in range(4):
                        bank = pT[ts % 2]
                        for cc in range(8):
                            def trf(e, s, ts=ts, cc=cc, bank=bank, buf=buf):
                                return e.transpose(out=bank[:, cc * 128:(cc + 1) * 128],
                                                   in_=x_bf[buf][:, ts * 1024 + cc * 128: ts * 1024 + (cc + 1) * 128],
                                                   identity=identb[:])
                            p.emit("tensor", trf, waits=[xcast_tok[buf][ts // 2], evac_tok[ts % 2]],
                                   inc=("sP", 1) if cc == 7 else None)
                        tok_tr = p.tok("sP")
                        def evx(e, s, ts=ts, bank=bank, buf=buf):
                            dv = x_fm[buf][:, :].rearrange("p (cc x) -> p cc x", x=512)[:, :, ts * 128:(ts + 1) * 128]
                            return e.tensor_copy(out=dv, in_=bank[:].rearrange("p (cc x) -> p cc x", x=128))
                        p.emit("vector", evx, waits=[tok_tr, xfm_vdone[buf]],
                               inc=("sV", 1))
                        evac_tok[ts % 2] = p.tok("sV")
                    xbf_trdone[buf] = tok_tr
                    xfm_ready = p.tok("sV")
                    xfm_ready_arr[buf] = xfm_ready
                    for ts in range(4):
                        for half in range(2):
                            bi = (ts * 2 + half) % 3
                            sgq = (ts * 2 + half) % 4
                            bank = pV[bi]
                            for cc in range(8):
                                def vmf(e, s, ts=ts, half=half, cc=cc, bank=bank, buf=buf):
                                    return e.matmul(out=bank[:],
                                                    lhsT=x_fm[buf][:, cc * 512 + ts * 128: cc * 512 + (ts + 1) * 128],
                                                    rhs=wTv[:, cc * 1024 + half * 512: cc * 1024 + (half + 1) * 512],
                                                    start=(cc == 0), stop=(cc == 7))
                                p.emit("tensor", vmf, waits=[xfm_ready, pv_tok[bi]],
                                       inc=("sP", 1) if cc == 7 else None)
                            tok_mm = p.tok("sP")
                            sgi = sgq
                            def evv(e, s, bank=bank, sgi=sgi):
                                return e.copy(out=stg[sgi][:], in_=bank[:])
                            p.emit("scalar", evv, waits=[tok_mm, stg_tok[sgi]], inc=("sC", 1))
                            pv_tok[bi] = p.tok("sC")
                            def stv(e, s, ts=ts, half=half, m=m, sgi=sgi):
                                sc = m * 4 + ts
                                dstv = v_scr[half * 4:(half + 1) * 4, :, sc, :, :]
                                dstv = dstv.transpose([2, 0, 1, 3])
                                return e.dma_start(out=dstv, in_=stg[sgi][:].rearrange(
                                    "p (pr h d) -> p pr h d", pr=4, h=2))
                            p.emit("sync", stv, waits=[pv_tok[bi]], inc=(f"sg{sgi}", 16))
                            stg_tok[sgi] = p.tok(f"sg{sgi}")
                    xfm_vdone[buf] = p.tok("sP")
                    if m == NMACRO - 1:
                        toks_V_stores.extend([p.tok(f"sg{i}") for i in range(4)])
                    if m == 1:
                        emit_qk_all("q", 0)
                        emit_qk_all("q", 1)
                    elif m >= 2:
                        emit_qk_all("q", m)
                        emit_qk_all("k", m)
                    if m + 1 < len(WORDER):
                        wprep_tr(WORDER[m + 1])

                toks_A_stores = ([p.tok(f"sg{i}") for i in range(4)]
                                 + [p.tok(f"xs{i}") for i in range(4)])
                tok_A_pe = p.tok("sP")
                tok_A_c = p.tok("sC")

            with contextlib.ExitStack() as st2:
                E2 = st2.enter_context
                Kt = [E2(nc.sbuf_tensor(f"Kt{i}", [128, T], BF16)) for i in range(2)]
                Qt = [E2(nc.sbuf_tensor(f"Qt{i}", [128, 4096], BF16)) for i in range(2)]
                Vt = [E2(nc.sbuf_tensor(f"Vt{i}", [128, 32 * 130], BF16)) for i in range(2)]
                Pbuf = [E2(nc.sbuf_tensor(f"Pbuf{i}", [128, 1536], BF16)) for i in range(7)]
                bc = [E2(nc.sbuf_tensor(f"bc{i}", [128, 512], F32)) for i in range(2)]
                pS = [E2(nc.psum_tensor(f"pS{i}", [128, 1536], F32)) for i in range(2)]
                pPV = [E2(nc.psum_tensor(f"pPV{i}", [128, 512], F32)) for i in range(2)]

                for i in range(2):
                    p.emit("vector", lambda e, s, i=i: e.memset(
                        Vt[i][:, :].rearrange("p (k x) -> p k x", x=65)[:, :, 64:65], 1.0),
                        waits=[tok_A_pe, tok_A_c] + toks_A_stores,
                        inc=("sV", 1))
                tok_ones = p.tok("sV")

                vt_tok = [NOT, NOT]
                pair_free = [tok_A_pe, tok_A_pe]
                qk_done = [[NOT], [NOT]]
                ps_free = [NOT, NOT]
                pbuf_free = [NOT] * 7
                ppv_free = [[NOT], [NOT]]
                ostage_free = [NOT, NOT, NOT]
                lstage_free = [NOT] * 3
                bc_free = [NOT, NOT]
                bc_rot = [0]

                def load_vt(pp):
                    buf = pp % 2
                    w = (toks_V_stores if pp == 0 else toks_A_stores) + [pair_free[buf], tok_ones]
                    for h in range(2):
                        def vload(e, s, pp=pp, h=h, buf=buf):
                            dstv = Vt[buf][:, :].rearrange("p (k x) -> p k x", x=130)[:, :, h * 65: h * 65 + 64]
                            return e.dma_start(out=dstv, in_=v_scr[pp, h].transpose([1, 0, 2]))
                        p.emit("sync", vload, waits=w, inc=(f"pr{buf}", 16))
                    vt_tok[buf] = p.tok(f"pr{buf}")

                def load_qk(pp):
                    buf = pp % 2
                    wq_ = (qk0_toks["q"] if pp == 0 else toks_A_stores) + [pair_free[buf]]
                    wk_ = (qk0_toks["k"] if pp == 0 else toks_A_stores) + [pair_free[buf]]
                    def ql(e, s, pp=pp, buf=buf):
                        return e.dma_start(out=Qt[buf][:], in_=q_scr[pp])
                    p.emit("sync", ql, waits=wq_, inc=(f"xf{buf}", 16))
                    def kl(e, s, pp=pp, buf=buf):
                        return e.dma_start(out=Kt[buf][:], in_=k_scr[pp])
                    p.emit("sync", kl, waits=wk_, inc=(f"xg{buf}", 16))
                    qk_done[buf] = [p.tok(f"xf{buf}"), p.tok(f"xg{buf}")]

                load_vt(0)
                load_qk(0)

                class NormSched:
                    def __init__(self, pp, rls_toks, oc_toks):
                        self.pp = pp
                        self.rls_toks = rls_toks
                        self.oc_toks = oc_toks
                        self.row_toks = [None] * 16
                        self.bbs = [None] * 16
                        self.ji = 0
                    def done(self):
                        return self.ji >= 16
                    def emit_rld(self, idx):
                        h = idx // 8
                        bb = bc_rot[0] % 2
                        bc_rot[0] += 1
                        self.bbs[idx] = bb
                        def rld(e, s, pp=self.pp, idx=idx, bb=bb, h=h):
                            return e.dma_start(
                                out=bc[bb][h * 64:(h + 1) * 64, :],
                                in_=rl_scr[pp, idx * 512:(idx + 1) * 512].partition_broadcast(64))
                        p.emit("gpsimd", rld, waits=[self.rls_toks[idx], bc_free[bb]],
                               inc=(f"rb{bb}", 16))
                        self.row_toks[idx] = p.tok(f"rb{bb}")
                    def emit_one(self):
                        idx = self.ji
                        if self.rls_toks[idx] is None or self.oc_toks[idx] is None:
                            return False
                        if self.row_toks[idx] is None:
                            self.emit_rld(idx)
                        if (idx + 1 < 16 and self.rls_toks[idx + 1] is not None
                                and self.row_toks[idx + 1] is None):
                            self.emit_rld(idx + 1)
                        self.ji += 1
                        pp = self.pp
                        h, sb = divmod(idx, 8)
                        bb = self.bbs[idx]
                        def ntt(e, s, h=h, sb=sb, pp=pp, bb=bb):
                            sl = o_stage[pp % 3][h * 64:(h + 1) * 64, sb * 512:(sb + 1) * 512]
                            return e.tensor_tensor(out=sl, in0=sl,
                                                   in1=bc[bb][h * 64:(h + 1) * 64, :], op=ALU.mult)
                        p.emit("vector", ntt, waits=[self.row_toks[idx], self.oc_toks[idx]],
                               inc=("sV", 1))
                        bc_free[bb] = p.tok("sV")
                        ret = True
                        if self.done():
                            osem = f"os{pp % 3}"
                            tok_n = p.tok("sV")
                            def ost(e, s, pp=pp):
                                return e.dma_start(out=o_scr[pp], in_=o_stage[pp % 3][:])
                            p.emit("gpsimd", ost, waits=[tok_n], inc=(osem, 16))
                            ostage_free[pp % 3] = p.tok(osem)
                        return ret

                deferred_chain = [None]
                for pp in range(NPAIR):
                    buf = pp % 2
                    scheds = []
                    if pp + 1 < NPAIR:
                        load_vt(pp + 1)
                        load_qk(pp + 1)
                    if deferred_chain[0] is not None:
                        nsched = deferred_chain[0]()
                        deferred_chain[0] = None
                        scheds.append(nsched)
                    lw = [vt_tok[buf]] + qk_done[buf]
                    rls_toks = [None] * 16
                    oc_toks = [None] * 16
                    if pp == NPAIR - 1:
                        scheds.append(NormSched(pp, rls_toks, oc_toks))
                    pend = []
                    s_count = 0

                    def emit_pv(job):
                        (sb, h, c0, ncg, gb, pb, etok, vtok, nchunk, is_last) = job
                        bank = (sb * 2 + h) % 2
                        for ci in range(ncg):
                            c = c0 + ci
                            def pvf(e, s, c=c, ci=ci, pb=pb, h=h, buf=buf, nchunk=nchunk, bank=bank):
                                return e.matmul(
                                    out=pPV[bank][0:65, :],
                                    lhsT=Vt[buf][:, c * 130 + h * 65: c * 130 + (h + 1) * 65],
                                    rhs=Pbuf[pb][:, ci * 512:(ci + 1) * 512],
                                    start=(c == 0), stop=(c == nchunk - 1))
                            waits = [etok] + ppv_free[bank]
                            if vtok is not None:
                                waits.append(vtok)
                            p.emit("tensor", pvf, waits=waits,
                                   inc=("sP", 1) if ci == ncg - 1 else None)
                        pbuf_free[pb] = p.tok("sP")
                        if is_last:
                            tok_pv = p.tok("sP")
                            lsb = (sb * 2 + h) % 3
                            def rcp(e, s, lsb=lsb, bank=bank):
                                return e.reciprocal(out=l_stage[lsb][:], in_=pPV[bank][64:65, :])
                            p.emit("vector", rcp, waits=[tok_pv, lstage_free[lsb]], inc=("sV", 1))
                            tok_l = p.tok("sV")
                            def ocp(e, s, h=h, sb=sb, obuf=pp % 3, bank=bank):
                                return e.tensor_copy(
                                    out=o_stage[obuf][h * 64:(h + 1) * 64, sb * 512:(sb + 1) * 512],
                                    in_=pPV[bank][0:64, :])
                            p.emit("vector", ocp, waits=[tok_pv, ostage_free[pp % 3]], inc=("sV", 1))
                            oc_toks[h * 8 + sb] = p.tok("sV")
                            ppv_free[bank] = [oc_toks[h * 8 + sb]]
                            def lst(e, s, pp=pp, h=h, sb=sb, lsb=lsb):
                                return e.dma_start(
                                    out=rl_scr[pp, (h * 8 + sb) * 512:(h * 8 + sb + 1) * 512].unsqueeze(0),
                                    in_=l_stage[lsb][0:1, :])
                            p.emit("gpsimd", lst, waits=[tok_l], inc=(f"ls{lsb}", 16))
                            lstage_free[lsb] = p.tok(f"ls{lsb}")
                            rls_toks[h * 8 + sb] = p.tok(f"ls{lsb}")

                    gidx = 0
                    for sb in range(NMACRO):
                        nchunk = 4 * sb + 4
                        ngroups = (nchunk + 2) // 3
                        for h in range(2):
                            r0, r1 = h * 64, (h + 1) * 64
                            for g in range(ngroups):
                                c0 = g * 3
                                ncg = min(3, nchunk - c0)
                                gb = gidx % 2
                                pb = gidx % 7
                                gidx += 1
                                for ci in range(ncg):
                                    c = c0 + ci
                                    def smf(e, s, c=c, ci=ci, gb=gb, sb=sb, r0=r0, r1=r1, buf=buf):
                                        return e.matmul(
                                            out=pS[gb][:, ci * 512:(ci + 1) * 512],
                                            lhsT=Kt[buf][r0:r1, c * 128:(c + 1) * 128],
                                            rhs=Qt[buf][r0:r1, sb * 512:(sb + 1) * 512],
                                            start=True, stop=True)
                                    p.emit("tensor", smf, waits=lw + [ps_free[gb]],
                                           inc=("sP", 1) if ci == ncg - 1 else None)
                                tok_s = p.tok("sP")
                                def expf(e, s, gb=gb, pb=pb, ncg=ncg):
                                    return e.activation(
                                        out=Pbuf[pb][:, 0:ncg * 512], in_=pS[gb][:, 0:ncg * 512],
                                        func=AF.Exp, bias=c_negC(), scale=c_sqk())
                                p.emit("scalar", expf,
                                       waits=[tok_s, pbuf_free[pb], tok_cmemset], inc=("sC", 1))
                                ps_free[gb] = p.tok("sC")
                                etok = p.tok("sC")
                                vtok = None
                                for ci in range(ncg):
                                    c = c0 + ci
                                    j = c - 4 * sb
                                    if j < 0:
                                        continue
                                    if j > 0:
                                        def msf(e, s, pb=pb, ci=ci, j=j):
                                            return e.memset(Pbuf[pb][:, ci * 512: ci * 512 + j * 128], 0.0)
                                        p.emit("gpsimd", msf, waits=[etok], inc=("sG", 1))
                                    def trif(e, s, pb=pb, ci=ci, j=j):
                                        sl = Pbuf[pb][:, ci * 512 + j * 128: ci * 512 + (j + 1) * 128]
                                        return e.tensor_tensor(out=sl, in0=sl, in1=tri[:], op=ALU.mult)
                                    p.emit("gpsimd", trif, waits=[etok], inc=("sG", 1))
                                    vtok = p.tok("sG")
                                pend.append((sb, h, c0, ncg, gb, pb, etok, vtok, nchunk,
                                             g == ngroups - 1))
                                if len(pend) >= 7:
                                    emit_pv(pend.pop(0))
                                    emit_pv(pend.pop(0))
                                s_count += 1
                                if s_count % 3 == 0:
                                    si = s_count // 3
                                    sc = scheds[si % len(scheds)] if scheds else None
                                    if sc is not None and sc.done():
                                        sc = next((x for x in scheds if not x.done()), None)
                                    if sc is not None and not sc.done():
                                        if not sc.emit_one():
                                            alt = next((x for x in scheds
                                                        if x is not sc and not x.done()), None)
                                            if alt is not None:
                                                alt.emit_one()
                    while pend:
                        emit_pv(pend.pop(0))
                    for sc in scheds:
                        while not sc.done():
                            sc.emit_one()
                    pair_free[buf] = p.tok("sP")

                    def chain(pp=pp, rls_toks=rls_toks, oc_toks=oc_toks):
                        return NormSched(pp, rls_toks, oc_toks)
                    if pp + 1 < NPAIR:
                        deferred_chain[0] = chain

                toks_phase2_stores = [p.tok("os0"), p.tok("os1"), p.tok("os2")]
                tok_phase2_pe = p.tok("sP")
                if debug_taps:
                    p.emit("gpsimd", lambda e, s: e.dma_start(out=taps["t_v"], in_=v_scr),
                           waits=toks_A_stores, inc=("tp", 16))
                    p.emit("gpsimd", lambda e, s: e.dma_start(out=taps["t_o"], in_=o_scr),
                           waits=toks_phase2_stores, inc=("tp", 16))
                    p.emit("gpsimd", lambda e, s: e.dma_start(out=taps["t_l"], in_=rl_scr),
                           waits=toks_phase2_stores, inc=("tp", 16))

        with contextlib.ExitStack() as st3:
            E3 = st3.enter_context
            o_fm = [E3(nc.sbuf_tensor(f"o_fm{i}", [128, 4096], BF16)) for i in range(2)]
            xr_f32 = [E3(nc.sbuf_tensor(f"xre{i}", [128, 1024], F32)) for i in range(2)]
            y1_f32 = [E3(nc.sbuf_tensor(f"y1f{i}", [128, 4096], F32)) for i in range(2)]
            y1_bf = E3(nc.sbuf_tensor("y1b_all", [128, 4096], BF16))
            y1_fm = E3(nc.sbuf_tensor("y1_fm", [128, 8 * 513], BF16))
            dmix = E3(nc.sbuf_tensor("dmix", [128, 4096], BF16))
            xr_bf = E3(nc.sbuf_tensor("xrb", [128, 4096], BF16))
            xk_bf = E3(nc.sbuf_tensor("xkb", [128, 4096], BF16))
            kact = dmix
            ktmp = E3(nc.sbuf_tensor("ktmp", [128, 512], F32))
            junk2 = E3(nc.sbuf_tensor("junk2", [128, 512], F32))
            r_tm = [E3(nc.sbuf_tensor(f"rtm{i}", [128, 1024], F32)) for i in range(2)]
            stats = E3(nc.sbuf_tensor("stats", [128, 56], F32))
            y2st = [E3(nc.sbuf_tensor(f"y2st{i}", [128, 1024], F32)) for i in range(2)]
            pM = [E3(nc.psum_tensor(f"pM{i}", [128, 512], F32)) for i in range(6)]
            pT2 = [E3(nc.psum_tensor(f"pT2_{i}", [128, 1024], BF16)) for i in range(2)]

            ofm_tok = [NOT, NOT]
            ofm_free = [NOT, NOT]
            xf_tok = [NOT, NOT]
            xf_free = [NOT, NOT]
            pm_free = [NOT] * 6
            pt2_free = [NOT, NOT]
            y1b_free = [NOT] * 4
            y2_free = [NOT, NOT]
            rtm_free = [NOT, NOT]
            y1f_free = [NOT, NOT]
            y1fm_mix_tok = NOT
            ffn_pe_done = NOT
            zf_arr = [[NOT] * 4 for _ in range(NMACRO)]
            rstd1_arr = [NOT] * NMACRO
            y1bf_arr = [[NOT] * 4 for _ in range(NMACRO)]
            ln1var_arr = [NOT] * NMACRO
            z2fin = NOT
            y1fm_arr = [NOT] * NMACRO
            xkmix_arr = [NOT] * NMACRO
            xrmix_arr = [NOT] * NMACRO
            kact_done = [NOT] * NMACRO

            def ldo(e, s, m, buf):
                dstv = o_fm[buf][:].rearrange("p (oc t) -> p oc t", t=512)
                return e.dma_start(out=dstv,
                                   in_=o_scr[:, :, m * 512:(m + 1) * 512].transpose([1, 0, 2]))
            p.emit("sync", lambda e, s: ldo(e, s, 0, 0),
                   waits=toks_phase2_stores + [tok_phase2_pe], inc=("of0", 16))
            ofm_tok[0] = p.tok("of0")

            def mk_ln_helpers(base):
                def lns_sum(e, s):
                    return e.tensor_add(out=stats[:, base + 16:base + 20],
                                        in0=stats[:, base:base + 4], in1=stats[:, base + 4:base + 8])
                def lns_meansc(e, s):
                    return e.tensor_scalar(out=stats[:, base + 16:base + 20], in0=stats[:, base + 16:base + 20],
                                           scalar1=1.0 / 1024, scalar2=None, op0=ALU.mult)
                def lns_sumq(e, s):
                    return e.tensor_add(out=stats[:, base + 20:base + 24],
                                        in0=stats[:, base + 8:base + 12], in1=stats[:, base + 12:base + 16])
                def lns_ex2sc(e, s):
                    return e.tensor_scalar(out=stats[:, base + 20:base + 24], in0=stats[:, base + 20:base + 24],
                                           scalar1=1.0 / 1024, scalar2=None, op0=ALU.mult)
                def lns_m2(e, s):
                    return e.tensor_mul(out=stats[:, base + 24:base + 28],
                                        in0=stats[:, base + 16:base + 20], in1=stats[:, base + 16:base + 20])
                def lns_var(e, s):
                    return e.tensor_sub(out=stats[:, base + 20:base + 24],
                                        in0=stats[:, base + 20:base + 24], in1=stats[:, base + 24:base + 28])
                def lns_ln(e, s):
                    return e.activation(out=stats[:, base + 20:base + 24], in_=stats[:, base + 20:base + 24],
                                        func=AF.Ln, bias=c_eps())
                def lns_exp(e, s):
                    return e.activation(out=stats[:, base + 20:base + 24], in_=stats[:, base + 20:base + 24],
                                        func=AF.Exp, scale=c_nh())
                return [lns_sum, lns_meansc, lns_sumq, lns_ex2sc, lns_m2, lns_var, lns_ln, lns_exp]

            LN1H = mk_ln_helpers(0)
            LN2H = mk_ln_helpers(28)

            def emit_ln_dve(H, first_wait_tok):
                def vs():
                    t = p.tok("sV")
                    p.emit("vector", lambda e, s, t=t: e.wait_ge(s[t[0]], t[1]))
                p.emit("vector", lambda e, s, t=first_wait_tok: e.wait_ge(s[t[0]], t[1]))
                p.emit("vector", H[0], inc=("sV", 1))
                vs()
                p.emit("vector", H[1], inc=("sV", 1))
                p.emit("vector", H[2], inc=("sV", 1))
                vs()
                p.emit("vector", H[3], inc=("sV", 1))
                p.emit("vector", H[4], inc=("sV", 1))
                vs()
                p.emit("vector", H[5], inc=("sV", 1))
                return p.tok("sV")

            def emit_ln_act(H, tok_var):
                p.emit("scalar", H[6], waits=[tok_var], inc=("sC", 1))
                tln = p.tok("sC")
                p.emit("scalar", lambda e, s, t=tln: e.wait_ge(s[t[0]], t[1]))
                p.emit("scalar", H[7], inc=("sC", 1))
                return p.tok("sC")

            def emit_ln_stats(H, first_wait_tok):
                return emit_ln_act(H, emit_ln_dve(H, first_wait_tok))

            def emit_proj(m):
                buf = m % 2
                if m + 1 < NMACRO:
                    nb = 1 - buf
                    p.emit("sync", lambda e, s, m=m, nb=nb: ldo(e, s, m + 1, nb),
                           waits=toks_phase2_stores + [ofm_free[nb]], inc=(f"of{nb}", 16))
                    ofm_tok[nb] = p.tok(f"of{nb}")
                for ts in range(4):
                    tsb = ts % 2
                    def xre(e, s, m=m, ts=ts, tsb=tsb):
                        return e.dma_start(out=xr_f32[tsb][:],
                                           in_=x_d[m * 512 + ts * 128: m * 512 + (ts + 1) * 128, :])
                    p.emit("sync", xre, waits=[xf_free[tsb], tok_phase2_pe], inc=(f"xg{tsb}", 16))
                    xf_tok[tsb] = p.tok(f"xg{tsb}")
                    for half in range(2):
                        bi = (ts * 2 + half) % 3
                        bank = pM[bi]
                        for oc in range(8):
                            def amf(e, s, oc=oc, half=half, ts=ts, bank=bank, buf=buf):
                                return e.matmul(
                                    out=bank[:],
                                    lhsT=o_fm[buf][:, oc * 512 + ts * 128: oc * 512 + (ts + 1) * 128],
                                    rhs=wT["wp"][:, oc * 1024 + half * 512: oc * 1024 + (half + 1) * 512],
                                    start=(oc == 0), stop=(oc == 7))
                            p.emit("tensor", amf, waits=[ofm_tok[buf], pm_free[bi]],
                                   inc=("sP", 1) if oc == 7 else None)
                        tok_mm = p.tok("sP")
                        def zf(e, s, bank=bank, half=half, ts=ts, tsb=tsb, buf=buf):
                            return e.scalar_tensor_tensor(
                                out=y1_f32[buf][:, ts * 1024 + half * 512: ts * 1024 + (half + 1) * 512],
                                in0=bank[:], scalar=SPV,
                                in1=xr_f32[tsb][:, half * 512:(half + 1) * 512],
                                op0=ALU.mult, op1=ALU.add,
                                accum_out=stats[:, half * 4 + ts: half * 4 + ts + 1])
                        p.emit("vector", zf, waits=[tok_mm, xf_tok[tsb], y1f_free[buf]],
                               inc=("sV", 1))
                        pm_free[bi] = p.tok("sV")
                    zf_arr[m][ts] = p.tok("sV")
                    xf_free[tsb] = zf_arr[m][ts]
                ofm_free[buf] = p.tok("sP")

            def emit_tailB1(m):
                p.emit("vector", lambda e, s, t=y1fm_arr[m]: e.wait_ge(s[t[0]], t[1]))
                def dmf(e, s):
                    v1 = y1_fm[:, :].rearrange("p (cc x) -> p cc x", x=513)[:, :, 1:513]
                    v0 = y1_fm[:, :].rearrange("p (cc x) -> p cc x", x=513)[:, :, 0:512]
                    return e.tensor_sub(out=dmix[:].rearrange("p (cc x) -> p cc x", x=512),
                                        in0=v1, in1=v0)
                p.emit("vector", dmf, waits=[ffn_pe_done], inc=("sV", 1))
                for cc in range(8):
                    def mixf(e, s, cc=cc):
                        return e.scalar_tensor_tensor(
                            out=xk_bf[:, cc * 512:(cc + 1) * 512],
                            in0=dmix[:, cc * 512:(cc + 1) * 512], scalar=muk_fm[:, cc:cc + 1],
                            in1=y1_fm[:, cc * 513: cc * 513 + 512],
                            op0=ALU.mult, op1=ALU.add)
                    p.emit("vector", mixf, inc=("sV", 1) if cc == 7 else None)
                xkmix_arr[m] = p.tok("sV")

            def emit_tailB2(m):
                nonlocal y1fm_mix_tok
                for cc in range(8):
                    def mixf(e, s, cc=cc):
                        return e.scalar_tensor_tensor(
                            out=xr_bf[:, cc * 512:(cc + 1) * 512],
                            in0=dmix[:, cc * 512:(cc + 1) * 512], scalar=mur_fm[:, cc:cc + 1],
                            in1=y1_fm[:, cc * 513: cc * 513 + 512],
                            op0=ALU.mult, op1=ALU.add)
                    p.emit("vector", mixf, inc=("sV", 1) if cc == 7 else None)
                xrmix_arr[m] = p.tok("sV")
                def bnd(e, s):
                    v = y1_fm[:, :].rearrange("p (cc x) -> p cc x", x=513)
                    return e.tensor_copy(out=v[:, :, 0:1], in_=v[:, :, 512:513])
                p.emit("vector", bnd, inc=("sV", 1))
                y1fm_mix_tok = p.tok("sV")

            def emit_tailA1(m):
                buf = m % 2
                for ts in range(4):
                    for half in range(2):
                        def zq(e, s, half=half, ts=ts, buf=buf):
                            sl = y1_f32[buf][:, ts * 1024 + half * 512: ts * 1024 + (half + 1) * 512]
                            return e.scalar_tensor_tensor(
                                out=junk2[:], in0=sl, scalar=1.0, in1=sl,
                                op0=ALU.mult, op1=ALU.mult,
                                accum_out=stats[:, 8 + half * 4 + ts: 9 + half * 4 + ts])
                        p.emit("vector", zq,
                               waits=[zf_arr[m][ts]] if half == 0 else [], inc=("sV", 1))
                var_tok = emit_ln_dve(LN1H, p.tok("sV"))
                ln1var_arr[m] = var_tok
                if m == 0:
                    p.emit("vector", lambda e, s: e.memset(
                        y1_fm[:, :].rearrange("p (cc x) -> p cc x", x=513)[:, :, 0:1], 0.0),
                        inc=("sV", 1))
                    y1fm_arr[0] = p.tok("sV")

            def emit_ln1_finish(m):
                buf = m % 2
                tok_rstd1 = emit_ln_act(LN1H, ln1var_arr[m])
                rstd1_arr[m] = tok_rstd1
                for ts in range(4):
                    def y1f2(e, s, ts=ts, buf=buf):
                        sl = y1_f32[buf][:, ts * 1024:(ts + 1) * 1024]
                        return e.tensor_scalar(
                            out=sl, in0=sl, scalar1=stats[:, 16 + ts:17 + ts],
                            scalar2=stats[:, 20 + ts:21 + ts],
                            op0=ALU.subtract, op1=ALU.mult)
                    p.emit("vector", y1f2, waits=[tok_rstd1], inc=("sV", 1))
                    def y1b(e, s, ts=ts, buf=buf):
                        return e.tensor_copy(out=y1_bf[:, ts * 1024:(ts + 1) * 1024],
                                             in_=y1_f32[buf][:, ts * 1024:(ts + 1) * 1024])
                    p.emit("vector", y1b, waits=[y1b_free[ts]], inc=("sV", 1))
                    y1bf_arr[m][ts] = p.tok("sV")

            def emit_tailA2(m):
                for ts in range(4):
                    tsb = ts % 2
                    bankT = pT2[ts % 2]
                    for cc in range(8):
                        def trf2(e, s, cc=cc, bankT=bankT, ts=ts):
                            return e.transpose(out=bankT[:, cc * 128:(cc + 1) * 128],
                                               in_=y1_bf[:, ts * 1024 + cc * 128: ts * 1024 + (cc + 1) * 128],
                                               identity=identb[:])
                        p.emit("tensor", trf2,
                               waits=[y1bf_arr[m][ts], pt2_free[ts % 2], y1fm_mix_tok],
                               inc=("sP", 1) if cc == 7 else None)
                    tok_tr = p.tok("sP")
                    y1b_free[ts] = tok_tr
                    def evy(e, s, ts=ts, bankT=bankT):
                        dv = y1_fm[:, :].rearrange("p (cc x) -> p cc x", x=513)[:, :, 1 + ts * 128: 1 + (ts + 1) * 128]
                        return e.copy(out=dv, in_=bankT[:].rearrange("p (cc x) -> p cc x", x=128))
                    p.emit("scalar", evy, waits=[tok_tr], inc=("sC", 1))
                    pt2_free[ts % 2] = p.tok("sC")
                y1fm_arr[m] = p.tok("sC")

            def emit_stage2(m, mnext):
                nonlocal ffn_pe_done, z2fin
                buf = m % 2
                for dc in range(8):
                    bi = 2 + dc % 2
                    bank = pM[bi]
                    for cc in range(8):
                        def kmf(e, s, cc=cc, dc=dc, bank=bank):
                            return e.matmul(
                                out=bank[:],
                                lhsT=wT["wk2"][:, cc * 1024 + dc * 128: cc * 1024 + (dc + 1) * 128],
                                rhs=xk_bf[:, cc * 512:(cc + 1) * 512],
                                start=(cc == 0), stop=(cc == 7))
                        p.emit("tensor", kmf, waits=[xkmix_arr[m], pm_free[bi]],
                               inc=("sP", 1) if cc == 7 else None)
                    tok_mm = p.tok("sP")
                    def rlf(e, s, bank=bank):
                        return e.activation(out=ktmp[:], in_=bank[:], func=AF.Relu)
                    p.emit("scalar", rlf, waits=[tok_mm, z2fin], inc=("sC", 1))
                    pm_free[bi] = p.tok("sC")
                    trl = p.tok("sC")
                    p.emit("scalar", lambda e, s, t=trl: e.wait_ge(s[t[0]], t[1]))
                    def sqf(e, s, dc=dc):
                        return e.activation(out=kact[:, dc * 512:(dc + 1) * 512],
                                            in_=ktmp[:], func=AF.Square)
                    p.emit("scalar", sqf, inc=("sC", 1))
                tok_kact = p.tok("sC")
                kact_done[m] = tok_kact
                if mnext is not None:
                    emit_ln1_finish(mnext)
                z2_toks = [NOT] * 4
                r_tok = [NOT, NOT]
                for ts in range(4):
                    tsb = ts % 2
                    for half in range(2):
                        bi = 4 + half
                        bank = pM[bi]
                        for cc in range(8):
                            def rmf(e, s, cc=cc, ts=ts, half=half, bank=bank):
                                return e.matmul(
                                    out=bank[:],
                                    lhsT=xr_bf[:, cc * 512 + ts * 128: cc * 512 + (ts + 1) * 128],
                                    rhs=wT["wr"][:, cc * 1024 + half * 512: cc * 1024 + (half + 1) * 512],
                                    start=(cc == 0), stop=(cc == 7))
                            p.emit("tensor", rmf, waits=[xrmix_arr[m], pm_free[bi]],
                                   inc=("sP", 1) if cc == 7 else None)
                        tok_mm = p.tok("sP")
                        def sgf(e, s, bank=bank, half=half, tsb=tsb):
                            return e.activation(out=r_tm[tsb][:, half * 512:(half + 1) * 512],
                                                in_=bank[:], func=AF.Sigmoid)
                        p.emit("scalar", sgf, waits=[tok_mm, rtm_free[tsb]], inc=("sC", 1))
                        pm_free[bi] = p.tok("sC")
                    r_tok[tsb] = p.tok("sC")
                    for half in range(2):
                        bi = 2 + half
                        bank = pM[bi]
                        for dc in range(8):
                            def vmf2(e, s, dc=dc, ts=ts, half=half, bank=bank):
                                return e.matmul(
                                    out=bank[:],
                                    lhsT=kact[:, dc * 512 + ts * 128: dc * 512 + (ts + 1) * 128],
                                    rhs=wT["wv2"][:, dc * 1024 + half * 512: dc * 1024 + (half + 1) * 512],
                                    start=(dc == 0), stop=(dc == 7))
                            p.emit("tensor", vmf2, waits=[tok_kact, pm_free[bi]],
                                   inc=("sP", 1) if dc == 7 else None)
                        tok_mm = p.tok("sP")
                        def ff(e, s, bank=bank, half=half, tsb=tsb):
                            return e.tensor_mul(out=ktmp[:], in0=bank[:],
                                                in1=r_tm[tsb][:, half * 512:(half + 1) * 512])
                        p.emit("vector", ff, waits=[tok_mm, r_tok[tsb]], inc=("sV", 1))
                        pm_free[bi] = p.tok("sV")
                        def z2f(e, s, half=half, ts=ts, buf=buf):
                            sl = y1_f32[buf][:, ts * 1024 + half * 512: ts * 1024 + (half + 1) * 512]
                            return e.scalar_tensor_tensor(
                                out=sl, in0=ktmp[:], scalar=1.0, in1=sl,
                                op0=ALU.mult, op1=ALU.add,
                                accum_out=stats[:, 28 + half * 4 + ts: 29 + half * 4 + ts])
                        p.emit("vector", z2f, inc=("sV", 1))
                        def z2q(e, s, half=half, ts=ts, buf=buf):
                            sl = y1_f32[buf][:, ts * 1024 + half * 512: ts * 1024 + (half + 1) * 512]
                            return e.scalar_tensor_tensor(
                                out=junk2[:], in0=sl, scalar=1.0, in1=sl,
                                op0=ALU.mult, op1=ALU.mult,
                                accum_out=stats[:, 36 + half * 4 + ts: 37 + half * 4 + ts])
                        p.emit("vector", z2q, inc=("sV", 1))
                        z2_toks[ts] = p.tok("sV")
                    rtm_free[tsb] = z2_toks[ts]
                tok_rstd2 = emit_ln_stats(LN2H, z2_toks[3])
                for ts in range(4):
                    tsb = ts % 2
                    def y2f(e, s, ts=ts, tsb=tsb, buf=buf):
                        return e.tensor_scalar(
                            out=y2st[tsb][:],
                            in0=y1_f32[buf][:, ts * 1024:(ts + 1) * 1024],
                            scalar1=stats[:, 44 + ts:45 + ts],
                            scalar2=stats[:, 48 + ts:49 + ts],
                            op0=ALU.subtract, op1=ALU.mult)
                    p.emit("vector", y2f, waits=[tok_rstd2, y2_free[tsb]], inc=("sV", 1))
                    tok_y2 = p.tok("sV")
                    def yst(e, s, m=m, ts=ts, tsb=tsb):
                        return e.dma_start(out=y_d[m * 512 + ts * 128: m * 512 + (ts + 1) * 128, :],
                                           in_=y2st[tsb][:])
                    p.emit("gpsimd", yst, waits=[tok_y2], inc=(f"ys{tsb}", 16))
                    y2_free[tsb] = p.tok(f"ys{tsb}")
                y1f_free[buf] = p.tok("sV")
                ffn_pe_done = p.tok("sP")
                z2fin = z2_toks[3]

            for m in range(NMACRO + 1):
                if m >= 1:
                    emit_tailB1(m - 1)
                if m < NMACRO:
                    emit_proj(m)
                if m >= 1:
                    emit_tailB2(m - 1)
                if m < NMACRO:
                    emit_tailA1(m)
                if m == 0:
                    emit_ln1_finish(0)
                if m >= 1:
                    emit_stage2(m - 1, m if m < NMACRO else None)
                if m < NMACRO:
                    emit_tailA2(m)

            finals = [p.tok(sm) for sm in DMA_SEMS if p.cnt[sm] > 0]
            def final_wait(e, s, finals=finals):
                for sm, v in finals:
                    e.wait_ge(s[sm], v)
            p.emit("sync", final_wait)

        with contextlib.ExitStack() as stb:
            sems = {nm: stb.enter_context(nc.semaphore(nm)) for nm in p.sem_names()}
            block = stb.enter_context(nc.Block())
            block.sync(lambda e: p.run("sync", e, sems))
            block.tensor(lambda e: p.run("tensor", e, sems))
            block.vector(lambda e: p.run("vector", e, sems))
            block.scalar(lambda e: p.run("scalar", e, sems))
            block.gpsimd(lambda e: p.run("gpsimd", e, sems))
    return nc


def kernel(L0_attn_x, L0_attn_W_key, L0_attn_W_query, L0_attn_W_value, L0_attn_W_proj,
           L0_ffn_W_r, L0_ffn_W_k, L0_ffn_W_v, L0_ffn_mu_r, L0_ffn_mu_k):
    x = np.ascontiguousarray(np.asarray(L0_attn_x, dtype=np.float32))
    ws = {
        "wq": np.ascontiguousarray(np.asarray(L0_attn_W_query, np.float32)),
        "wk": np.ascontiguousarray(np.asarray(L0_attn_W_key, np.float32)),
        "wv": np.ascontiguousarray(np.asarray(L0_attn_W_value, np.float32)),
        "wp": np.ascontiguousarray(np.asarray(L0_attn_W_proj, np.float32)),
        "wr": np.ascontiguousarray(np.asarray(L0_ffn_W_r, np.float32)),
        "wk2": np.ascontiguousarray(np.asarray(L0_ffn_W_k, np.float32)),
        "wv2": np.ascontiguousarray(np.asarray(L0_ffn_W_v, np.float32)),
    }
    mu_r = np.ascontiguousarray(np.asarray(L0_ffn_mu_r, np.float32))
    mu_k = np.ascontiguousarray(np.asarray(L0_ffn_mu_k, np.float32))
    identb = np.eye(128, dtype=ml_dtypes.bfloat16)
    identf = np.eye(128, dtype=np.float32)
    tri = np.triu(np.ones((128, 128), dtype=ml_dtypes.bfloat16))

    nc = build_program()
    in_maps = []
    for b in range(8):
        mm = {"x": x[b], "mu_r": mu_r, "mu_k": mu_k,
              "identb": identb, "identf": identf, "tri": tri}
        mm.update(ws)
        in_maps.append(mm)
    res = run_bass_kernel_spmd(nc, in_maps, core_ids=list(range(8)))
    return np.stack([res.results[b]["y"] for b in range(8)], axis=0)
